# version 01 (final)
# baseline (speedup 1.0000x reference)
import numpy as np
import ml_dtypes
import concourse.bass as bass
import concourse.mybir as mybir
from concourse.bass_utils import run_bass_kernel_spmd
from contextlib import ExitStack

F32 = mybir.dt.float32
BF16 = mybir.dt.bfloat16
I32 = mybir.dt.int32
AF = mybir.ActivationFunctionType
ALU = mybir.AluOpType
AX = mybir.AxisListType.X
BF16NP = ml_dtypes.bfloat16

NCORES = 8
D = 1024
TLOC = 2048
NT = 8
INNER = 2048
NTI = 16
DI = 2048
DS_A = 64
NLB = 4
HID = 4096
NEXP = 4
NCH = 4

ENGINES = ("tensor", "vector", "scalar", "gpsimd", "sync")
N_STAGES = 4
DEBUG = []
DBG_SHAPES = {}


class Prog:
    def __init__(self, nc, block, ctx):
        self.nc = nc
        self.block = block
        self.ctx = ctx
        self.sem = {}
        self.val = {}
        self.is_dma_tag = set()
        for e in ENGINES:
            self.sem[e] = ctx.enter_context(nc.semaphore(f"sem_{e}"))
            self.val[e] = 0
        self.res = {}
        self.alias = {}
        self.waited = {e: {} for e in ENGINES}
        self._regn = 0

    def get_dma_sem(self, key):
        tag = "dma_" + key
        if tag not in self.sem:
            self.sem[tag] = self.ctx.enter_context(self.nc.semaphore(f"sem_{tag}"))
            self.val[tag] = 0
            self.is_dma_tag.add(tag)
        return tag

    def _waits_for(self, eng, reads, writes):
        need = {}

        def add(tag, v):
            if tag in self.is_dma_tag:
                v = self.val[tag]
            need[tag] = max(need.get(tag, 0), v)

        for r in reads:
            ent = self.res.get(r)
            if ent is not None and ent[0] is not None:
                add(*ent[0])
        for w_ in writes:
            ent = self.res.get(w_)
            if ent is None:
                continue
            if ent[0] is not None:
                add(*ent[0])
            for tag, v in ent[1].items():
                add(tag, v)
        out = []
        for tag, v in need.items():
            if self.waited[eng].get(tag, 0) >= v:
                continue
            out.append((tag, v))
        return out

    def _record(self, eng_tag, newval, reads, writes):
        for r in reads:
            ent = self.res.setdefault(r, [None, {}])
            ent[1][eng_tag] = max(ent[1].get(eng_tag, 0), newval)
        group = list(writes)
        for w in writes:
            self.res[w] = [(eng_tag, newval), {}]
            self.alias[w] = group

    def _expand_reads(self, reads):
        out = []
        seen = set()
        for r in reads:
            for k in self.alias.get(r, (r,)):
                if k not in seen:
                    seen.add(k)
                    out.append(k)
            if r not in seen:
                seen.add(r)
                out.append(r)
        return out

    def op(self, eng, emit, reads=(), writes=()):
        self.ops(eng, lambda h: [emit(h)], reads, writes)

    def ops(self, eng, emit_many, reads=(), writes=()):
        sec = getattr(self.block, eng)
        reads = self._expand_reads(list(reads))
        waits = self._waits_for(eng, reads, list(writes))
        tag = eng
        self.val[tag] += 1
        newval = self.val[tag]

        def body(h):
            for t, v in waits:
                h.wait_ge(self.sem[t], v)
                self.waited[eng][t] = max(self.waited[eng].get(t, 0), v)
            insts = emit_many(h)
            insts[-1].then_inc(self.sem[tag], 1)
        sec(body)
        self._record(tag, newval, list(reads), list(writes))

    def dma(self, queue, emit, reads=(), writes=(), sem_key=None):
        sec = getattr(self.block, queue)
        reads = self._expand_reads(list(reads))
        waits = self._waits_for(queue, reads, list(writes))
        tag = self.get_dma_sem(sem_key or queue)
        self.val[tag] += 16
        newval = self.val[tag]

        def body(h):
            for t, v in waits:
                h.wait_ge(self.sem[t], v)
                self.waited[queue][t] = max(self.waited[queue].get(t, 0), v)
            emit(h).then_inc(self.sem[tag], 16)
        sec(body)
        self._record(tag, newval, list(reads), list(writes))

    def dma_multi(self, queue, emit, n_per_path, reads=(), writes=(), sem_key=None):
        sec = getattr(self.block, queue)
        reads = self._expand_reads(list(reads))
        waits = self._waits_for(queue, reads, list(writes))
        tag = self.get_dma_sem(sem_key or queue)
        self.val[tag] += 16 * n_per_path
        newval = self.val[tag]
        sem = self.sem[tag]

        def body(h):
            for t, v in waits:
                h.wait_ge(self.sem[t], v)
                self.waited[queue][t] = max(self.waited[queue].get(t, 0), v)
            emit(h, lambda ins: ins.then_inc(sem, 16))
        sec(body)
        self._record(tag, newval, list(reads), list(writes))

    def barrier_wait(self, eng, tags=None):
        sec = getattr(self.block, eng)
        if tags is None:
            tags = [t for t in self.val if t != eng]
        todo = [(t, self.val[t]) for t in tags
                if self.val[t] > 0 and self.waited[eng].get(t, 0) < self.val[t]]

        def body(h):
            for t, v in todo:
                h.wait_ge(self.sem[t], v)
                self.waited[eng][t] = max(self.waited[eng].get(t, 0), v)
        sec(body)

    def regname(self, base):
        self._regn += 1
        return f"{base}_{self._regn}"


def build_program(n_stages=N_STAGES):
    nc = bass.Bass("TRN2", target_bir_lowering=False)
    ctx = ExitStack()
    EI = {}

    def ein(name, shape, dt=F32):
        EI[name] = nc.dram_tensor(name, shape, dt, kind="ExternalInput").ap()
        return EI[name]

    ein("x_fm", [D, TLOC])
    ein("x_halo", [D, 3])
    ein("lane", [1, 1], I32)
    ein("identity_bf", [128, 128], BF16)
    ein("IDNF", [128, 128])
    ein("lane_f", [128, 1])
    ein("a_norm1_w", [128, NT])
    ein("a_norm2_w", [128, NT])
    ein("a_in_proj_w", [D, 2 * INNER])
    ein("a_in_proj_b", [128, 2 * NTI])
    ein("a_in_proj_bh", [128, NTI])
    ein("a_conv_w", [128, NTI * 3])
    ein("a_conv_b", [128, NTI])
    ein("a_dtbc_w", [INNER, 192])
    ein("a_dtbc_b1", [128, 1])
    ein("a_dtbc_b2", [128, 1])
    ein("a_s2ib_w", [DS_A + 1, INNER])
    ein("a_D", [128, NTI])
    ein("a_out_wo", [NT, NTI, 128, 128])
    ein("a_out_b", [128, NT])
    ein("a_gate_w", [128, NT * NEXP])
    ein("a_gate_b", [1, NEXP])
    ein("a_e_w1", [NEXP, D, HID])
    ein("a_e_b1", [NEXP, 128, HID // 128])
    ein("a_e_w2o", [NEXP, NT, HID // 128, 128, 128])
    ein("a_e_b2", [NEXP, 128, NT])
    ein("f_selmat", [45, 96])
    ein("f_negthr", [128, 1])
    ein("f_node_w", [D, 45])
    ein("f_node_b", [128, 1])
    for _f in ("q", "k", "v"):
        ein(f"f_{_f}_w1", [D, 1024])
        ein(f"f_{_f}_b1", [128, 8])
        ein(f"f_{_f}_w2", [1024, D])
        ein(f"f_{_f}_b2", [16, D])
    ein("b_norm_w", [NLB, 128, NT])
    ein("b_norm_b", [NLB, 128, NT])
    ein("b_in_proj_w", [NLB, D, 2 * DI])
    ein("b_conv_w", [NLB, 128, NTI * 4])
    ein("b_conv_b", [NLB, 128, NTI])
    ein("b_x_proj_w", [NLB, DI, 96])
    ein("b_dt_proj_w", [NLB, 64, DI])
    ein("b_dt_proj_b", [NLB, 128, NTI])
    ein("b_A_log", [NLB, 128, NT * 16])
    ein("b_D", [NLB, 128, NTI])
    ein("b_out_proj_wo", [NLB, NT, NTI, 128, 128])
    ein("b_normf_w", [128, NT])
    ein("b_normf_b", [128, NT])

    xout = nc.dram_tensor("x_out", [D, TLOC], F32, kind="ExternalOutput").ap()

    xdr = nc.dram_tensor("xdr", [D, TLOC], F32).ap()
    resdr = nc.dram_tensor("resdr", [D, TLOC], F32).ap()
    spill = nc.dram_tensor("spill", [3 * INNER, TLOC], BF16).ap()
    bc = nc.dram_tensor("bc_rows", [8, 4096], F32).ap()
    selr = nc.dram_tensor("selr", [96, TLOC], BF16).ap()
    bar_in = nc.dram_tensor("bar_in", [1, 8], F32)
    bar_out = nc.dram_tensor("bar_out", [1, 8], F32)
    XW = 24 * 1024 * 1024
    xch = nc.dram_tensor("xch", [2, XW], F32, addr_space="Shared").ap()
    xchb = xch.bitcast(BF16)
    non = nc.dram_tensor("non", [2, 256], I32, addr_space="Shared").ap()

    X8 = [ctx.enter_context(nc.sbuf_tensor(f"Xa{i}", [128, 2048], F32)) for i in range(8)]
    A8 = [ctx.enter_context(nc.sbuf_tensor(f"Aa{i}", [128, 4104], BF16)) for i in range(8)]
    B8 = [ctx.enter_context(nc.sbuf_tensor(f"Ba{i}", [128, 4096], BF16)) for i in range(7)]
    PBUF = ctx.enter_context(nc.sbuf_tensor("PBUF", [128, 4096], BF16))
    IDN = ctx.enter_context(nc.sbuf_tensor("IDN", [128, 128], BF16))
    IDNF = ctx.enter_context(nc.sbuf_tensor("IDNFs", [128, 128], F32))
    VEC = ctx.enter_context(nc.sbuf_tensor("VEC", [128, 448], F32))
    EVa = ctx.enter_context(nc.sbuf_tensor("EVa", [128, 512], F32))
    EVb = ctx.enter_context(nc.sbuf_tensor("EVb", [128, 512], F32))
    GS = [ctx.enter_context(nc.sbuf_tensor(f"GSs{i}", [128, 512], BF16)) for i in range(3)]
    ZEROI = ctx.enter_context(nc.sbuf_tensor("ZEROI", [1, 256], I32))
    NONCE = ctx.enter_context(nc.sbuf_tensor("NONCEs", [1, 16], I32))
    LANE = ctx.enter_context(nc.sbuf_tensor("LANEs", [1, 1], I32))
    ONESF = ctx.enter_context(nc.sbuf_tensor("ONESFs", [1, 8], F32))
    ONESB = ctx.enter_context(nc.sbuf_tensor("ONESBs", [128, 1], BF16))

    B2S = ctx.enter_context(nc.sbuf_tensor("B2S", [80, 1024], BF16))
    NEAT = ctx.enter_context(nc.sbuf_tensor("NEAT", [128, 128], F32))
    CWS = ctx.enter_context(nc.sbuf_tensor("CWS", [128, 64], F32))
    ONER = ctx.enter_context(nc.sbuf_tensor("ONER", [1, 128], BF16))
    NWS = ctx.enter_context(nc.sbuf_tensor("NWS", [128, 368], F32))
    SELW = ctx.enter_context(nc.sbuf_tensor("SELW", [45, 96], BF16))
    PSQ = [ctx.enter_context(nc.psum_tensor(f"psq{i}", [128, 1024], F32)) for i in range(4)]
    BK = ["bank0", "bank1", "bank2", "bank3"]

    block = ctx.enter_context(nc.Block())
    p = Prog(nc, block, ctx)

    def f32v(t):
        return t.ap().bitcast(F32)

    def b16v(t):
        return t.ap().bitcast(BF16)

    XRES = [f"X{i}" for i in range(8)]
    ARES = [f"A{i}" for i in range(8)]
    BRES = [f"B{i}" for i in range(7)]

    p.op("vector", lambda h: h.memset(ZEROI[:, :], 0), writes=["ZEROI"])
    p.op("vector", lambda h: h.memset(NONCE[:, :], 1), writes=["NONCE"])
    p.op("vector", lambda h: h.memset(ONESF[:, :], 1.0), writes=["ONESF"])
    p.op("vector", lambda h: h.memset(ONESB[:, :], 1.0), writes=["ONESB"])
    p.op("vector", lambda h: h.memset(ONER[:, :], 1.0), writes=["ONER"])
    p.dma("sync", lambda h: h.dma_start(out=LANE[:, :], in_=EI["lane"]), writes=["LANE"])
    p.dma("sync", lambda h: h.dma_start(out=IDN[:, :], in_=EI["identity_bf"]), writes=["IDN"])
    p.dma("sync", lambda h: h.dma_start(out=IDNF[:, :], in_=EI["IDNF"]), writes=["IDNF"])
    p.dma("gpsimd", lambda h: h.dma_start(out=xdr, in_=EI["x_fm"]), writes=["xdr"], sem_key="hsk")

    lane_reg = {}

    def get_lane(h):
        if "r" not in lane_reg:
            lane_reg["r"] = h.register("lane_hold").__enter__()
            h.load(lane_reg["r"], LANE[0:1, 0:1])
        return lane_reg["r"]

    def emit_clear_barrier():
        def emit(h, inc):
            lr = get_lane(h)
            with h.If_eq(lr, 0):
                inc(h.dma_start(out=non[0:1, :], in_=ZEROI[0:1, :]))
            with h.Else():
                inc(h.dma_start(out=non[1:2, :], in_=ZEROI[0:1, :]))
        p.dma_multi("gpsimd", emit, 1, reads=["ZEROI", "LANE"], writes=["nonce_clear"], sem_key="hsk")
        p.dma("gpsimd", lambda h: h.dma_start(out=bar_in.ap(), in_=ONESF[:, :]),
              reads=["ONESF"], writes=["bar_in"], sem_key="hsk")

        def barr(h):
            return h.collective_compute(
                "AllReduce", ALU.add, replica_groups=[list(range(NCORES))],
                ins=[bar_in.ap().opt()], outs=[bar_out.ap().opt()])
        p.op("gpsimd", barr, reads=["bar_in", "nonce_clear"], writes=["barrier"])
    emit_clear_barrier()

    def wload(dst_view, src_ap, tag, cast=True, dst_res=None):
        q = "gpsimd" if cast else "sync"
        p.dma(q, lambda h: h.dma_start(out=dst_view, in_=src_ap),
              writes=[tag] + ([dst_res] if dst_res else []), sem_key="wl")

    bccnt = [0]

    def bcast_row(src_row_sbuf, dst_tile, nparts, tag, deps, width=TLOC, dst_res=None):
        r = bccnt[0] % 4
        bccnt[0] += 1
        p.dma("gpsimd", lambda h: h.dma_start(out=bc[r:r + 1, 0:width], in_=src_row_sbuf),
              reads=deps, writes=[f"bcd_{tag}", f"bcrow{r}"], sem_key="bc")
        p.dma("gpsimd", lambda h: h.dma_start(
            out=dst_tile, in_=bc[r:r + 1, 0:width].broadcast_to([nparts, width])),
            reads=[f"bcd_{tag}"], writes=[tag] + ([dst_res] if dst_res else []),
            sem_key="bc")

    def rsqrt_inplace(row_ap, eps, tag, deps, row_res):
        ec = const_col(eps)
        hc = const_col(-0.5)
        p.op("scalar", lambda h: h.activation(out=row_ap, in_=row_ap, func=AF.Ln,
                                              bias=ec[0:1, 0:1]),
             reads=deps + [row_res, "gb_vecs"], writes=[f"{tag}_ln", row_res])
        p.op("scalar", lambda h: h.activation(out=row_ap, in_=row_ap, func=AF.Exp,
                                              scale=hc[0:1, 0:1]),
             reads=[f"{tag}_ln", "gb_vecs"], writes=[tag, row_res])

    xoff = [0]
    xsite = [0]
    pair_sites = {}

    def xalloc(words):
        o = xoff[0]
        xoff[0] += words
        assert xoff[0] <= XW
        return o

    def lane_write(tag, pairs_fn, deps):
        n_per = len(pairs_fn(0))
        assert len(pairs_fn(1)) == n_per

        def emit(h, inc):
            lr = get_lane(h)
            with h.If_eq(lr, 0):
                for o, i in pairs_fn(0):
                    inc(h.dma_start(out=o, in_=i))
            with h.Else():
                for o, i in pairs_fn(1):
                    inc(h.dma_start(out=o, in_=i))
        p.dma_multi("gpsimd", emit, n_per, reads=list(deps) + ["barrier"],
                    writes=[f"xw_{tag}"], sem_key="hsk")

    def nonce_write(tag, deps):
        site = xsite[0]
        xsite[0] += 1
        assert site < 256
        pair_sites[tag] = site

        def emit_n(h, inc):
            lr = get_lane(h)
            with h.If_eq(lr, 0):
                inc(h.dma_start(out=non[0:1, site:site + 1], in_=NONCE[0:1, 0:1]))
            with h.Else():
                inc(h.dma_start(out=non[1:2, site:site + 1], in_=NONCE[0:1, 0:1]))
        p.dma_multi("gpsimd", emit_n, 1, reads=list(deps) + ["NONCE", "barrier"],
                    writes=[f"xn_{tag}"], sem_key="hsk")

    def pair_write(tag, pairs_fn, deps):
        lane_write(tag, pairs_fn, deps)
        nonce_write(tag, [f"xw_{tag}"])

    def pair_poll_read(tag, pairs_fn, writes, extra_deps=()):
        site = pair_sites[tag]

        n_per = len(pairs_fn(0))
        assert len(pairs_fn(1)) == n_per

        def emit(h, inc):
            lr = get_lane(h)
            n1, n2 = p.regname("cur"), p.regname("dd")
            with h.register(n1) as cur, h.register(n2) as d:
                with h.If_eq(lr, 0):
                    h.load(cur, non[1:2, site:site + 1])
                    h.reg_sub(d, cur, 1)
                    with h.While(d):
                        h.load(cur, non[1:2, site:site + 1])
                        h.reg_sub(d, cur, 1)
                    for o, i in pairs_fn(0):
                        inc(h.dma_start(out=o, in_=i))
                with h.Else():
                    h.load(cur, non[0:1, site:site + 1])
                    h.reg_sub(d, cur, 1)
                    with h.While(d):
                        h.load(cur, non[0:1, site:site + 1])
                        h.reg_sub(d, cur, 1)
                    for o, i in pairs_fn(1):
                        inc(h.dma_start(out=o, in_=i))
        p.dma_multi("gpsimd", emit, n_per, reads=["barrier"] + list(extra_deps),
                    writes=list(writes), sem_key="hsk")

    vecmap = {}

    def load_vec(name, src_ap, cols):
        col0 = load_vec.next
        load_vec.next += cols
        assert load_vec.next <= 448
        p.dma("sync", lambda h: h.dma_start(out=VEC[:, col0:col0 + cols], in_=src_ap),
              writes=["gb_vecs"], sem_key="wl2")
        vecmap[name] = col0
    load_vec.next = 0

    def valloc(name, cols):
        col0 = load_vec.next
        load_vec.next += cols
        assert load_vec.next <= 448
        vecmap[name] = col0
        return VEC[:, col0:col0 + cols]

    def vcol(name, j=0):
        return VEC[:, vecmap[name] + j:vecmap[name] + j + 1]

    def vcols(name, j, n):
        return VEC[:, vecmap[name] + j:vecmap[name] + j + n]

    _ccache = {}

    def const_col(val):
        if val not in _ccache:
            col = valloc(f"const_{len(_ccache)}", 1)
            p.op("vector", (lambda col=col, val=val: lambda h: h.memset(col, val))(),
                 writes=["gb_vecs"])
            _ccache[val] = col
        return _ccache[val]

    dbg_n = [0]

    def dbg_dram(tag, src_ap, res, rows=D):
        if tag not in DEBUG:
            return
        k = dbg_n[0]
        dbg_n[0] += 1
        o = nc.dram_tensor(f"dbg{k}", [rows, TLOC], F32, kind="ExternalOutput").ap()
        DBG_SHAPES[tag] = f"dbg{k}"
        p.dma("gpsimd", lambda h: h.dma_start(out=o, in_=src_ap), reads=[res],
              writes=[f"dbg_{tag}"], sem_key="dbg")

    def dbg_sbuf(tag, ap_list, res_list, rows=128, regions=()):
        if tag not in DEBUG:
            return
        k = dbg_n[0]
        dbg_n[0] += 1
        nt = len(ap_list)
        o = nc.dram_tensor(f"dbg{k}", [nt * rows, TLOC], F32, kind="ExternalOutput").ap()
        DBG_SHAPES[tag] = f"dbg{k}"
        for i, (a, r) in enumerate(zip(ap_list, res_list)):
            p.dma("gpsimd", (lambda i=i, a=a: lambda h: h.dma_start(
                out=o[i * rows:(i + 1) * rows, :], in_=a))(),
                reads=[r] + list(regions),
                writes=[f"dbg_{tag}{i}"], sem_key="dbg")

    hbf = [B8[i // 2].ap()[:, (i % 2) * 2048:(i % 2) * 2048 + 2048] for i in range(NT)]
    HBR = [BRES[i // 2] for i in range(NT)]

    def emit_norm(src_dram, src_res, tag, kind, gamma=None, beta=None, eps=1e-5,
                  t2_cb=None, fp32_dst=None, fp32_res=None):
        xs = [f32v(X8[6])[:, 0:2048], f32v(X8[7])[:, 0:2048]]
        xsr = ["X6", "X7"]
        xbf = [b16v(X8[5])[:, 0:2048], b16v(X8[5])[:, 2048:4096]]
        xsq = [b16v(X8[4])[:, 0:2048], b16v(X8[4])[:, 2048:4096]]
        stat = f32v(X8[2])[0:1, 0:2048]
        stat2 = f32v(X8[3])[0:1, 0:2048]
        stat3 = f32v(X8[5])[0:1, 0:2048]

        def sum_region(c):
            return (c // 2, (c % 2) * 512)

        def sq_region(c):
            return (2 + c // 2, (c % 2) * 512)

        for i in range(NT):
            st = xs[i % 2]
            rx = f"{tag}_x{i}"
            p.dma("sync", (lambda st=st, i=i: lambda h: h.dma_start(
                out=st, in_=src_dram[i * 128:(i + 1) * 128, :]))(),
                reads=[src_res], writes=[rx, xsr[i % 2]], sem_key="xs")
            p.op("vector", (lambda st=st, i=i: lambda h: h.tensor_copy(
                out=xbf[i % 2], in_=st))(),
                reads=[rx], writes=[f"{tag}_xbf{i}", "X5"])
            p.op("scalar", (lambda i=i: lambda h: h.activation(
                out=xsq[i % 2], in_=xbf[i % 2], func=AF.Square))(),
                reads=[f"{tag}_xbf{i}"], writes=[f"{tag}_xsq{i}", "X4"])
            for c in range(NCH):
                if kind == "ln":
                    qi, c0 = sum_region(c)
                    p.op("tensor", (lambda i=i, c=c, qi=qi, c0=c0: lambda h: h.matmul(
                        out=PSQ[qi][0:1, c0:c0 + 512], lhsT=ONESB[:, :],
                        rhs=xbf[i % 2][:, c * 512:(c + 1) * 512],
                        start=(i == 0), stop=(i == NT - 1)))(),
                        reads=[f"{tag}_xbf{i}", "ONESB"], writes=[BK[qi]])
                qi, c0 = sq_region(c)
                p.op("tensor", (lambda i=i, c=c, qi=qi, c0=c0: lambda h: h.matmul(
                    out=PSQ[qi][0:1, c0:c0 + 512], lhsT=ONESB[:, :],
                    rhs=xsq[i % 2][:, c * 512:(c + 1) * 512],
                    start=(i == 0), stop=(i == NT - 1)))(),
                    reads=[f"{tag}_xsq{i}", "ONESB"], writes=[BK[qi]])
        for c in range(NCH):
            if kind == "ln":
                qi, c0 = sum_region(c)
                p.op("vector", (lambda c=c, qi=qi, c0=c0: lambda h: h.tensor_scalar(
                    out=stat[0:1, c * 512:(c + 1) * 512], in0=PSQ[qi][0:1, c0:c0 + 512],
                    scalar1=1.0 / D, scalar2=None, op0=ALU.mult))(),
                    reads=[BK[qi]], writes=[f"{tag}_mu{c}", "X2"])
            qi, c0 = sq_region(c)
            p.op("vector", (lambda c=c, qi=qi, c0=c0: lambda h: h.tensor_scalar(
                out=stat2[0:1, c * 512:(c + 1) * 512], in0=PSQ[qi][0:1, c0:c0 + 512],
                scalar1=1.0 / D, scalar2=None, op0=ALU.mult))(),
                reads=[BK[qi]], writes=[f"{tag}_vr{c}", "X3"])
        vr = [f"{tag}_vr{c}" for c in range(NCH)]
        if kind == "ln":
            mu = [f"{tag}_mu{c}" for c in range(NCH)]
            p.op("scalar", lambda h: h.activation(out=stat3, in_=stat, func=AF.Square),
                 reads=mu, writes=[f"{tag}_mu2", "X5"])
            p.op("vector", lambda h: h.tensor_sub(out=stat2, in0=stat2, in1=stat3),
                 reads=[f"{tag}_mu2"] + vr, writes=[f"{tag}_var", "X3"])
            rsqrt_inplace(stat2, eps, f"{tag}_inv", [f"{tag}_var"], "X3")
            bcast_row(stat, f32v(X8[2])[:, 0:2048], 128, f"{tag}_murep", mu + ["X2"],
                      dst_res="X2")
            bcast_row(stat2, f32v(X8[3])[:, 0:2048], 128, f"{tag}_invrep",
                      [f"{tag}_inv", "X3"], dst_res="X3")
        else:
            rsqrt_inplace(stat2, eps, f"{tag}_inv", vr, "X3")
            bcast_row(stat2, f32v(X8[3])[:, 0:2048], 128, f"{tag}_invrep",
                      [f"{tag}_inv", "X3"], dst_res="X3")
        murep = f32v(X8[2])[:, 0:2048]
        invrep = f32v(X8[3])[:, 0:2048]
        t2s = [f32v(X8[5])[:, 0:2048], f32v(X8[4])[:, 0:2048]]
        t2r = ["X5", "X4"]
        for i in range(NT):
            st = xs[i % 2]
            rx = f"{tag}_y{i}"
            p.dma("sync", (lambda st=st, i=i: lambda h: h.dma_start(
                out=st, in_=src_dram[i * 128:(i + 1) * 128, :]))(),
                reads=[src_res], writes=[rx, xsr[i % 2]], sem_key="xs")
            t2 = t2s[i % 2]
            rt2 = f"{tag}_t2_{i}"
            if kind == "ln":
                p.op("vector", (lambda st=st, t2=t2: lambda h: h.tensor_sub(
                    out=t2, in0=st, in1=murep))(),
                    reads=[rx, f"{tag}_murep"], writes=[f"{tag}_sb{i}", t2r[i % 2]])
                p.op("vector", (lambda t2=t2: lambda h: h.tensor_mul(
                    out=t2, in0=t2, in1=invrep))(),
                    reads=[f"{tag}_sb{i}", f"{tag}_invrep"], writes=[rt2, t2r[i % 2]])
            else:
                p.op("vector", (lambda st=st, t2=t2: lambda h: h.tensor_mul(
                    out=t2, in0=st, in1=invrep))(),
                    reads=[rx, f"{tag}_invrep"], writes=[rt2, t2r[i % 2]])
            hrw = [f"{tag}_hbf{i}", HBR[i]]
            if gamma is not None:
                if beta is not None:
                    p.op("scalar", (lambda i=i, t2=t2: lambda h: h.activation(
                        out=hbf[i], in_=t2, func=AF.Identity,
                        scale=gamma(i), bias=beta(i)))(),
                        reads=[rt2, "gb_vecs"], writes=hrw)
                else:
                    p.op("scalar", (lambda i=i, t2=t2: lambda h: h.activation(
                        out=hbf[i], in_=t2, func=AF.Identity, scale=gamma(i)))(),
                        reads=[rt2, "gb_vecs"], writes=hrw)
            else:
                p.op("vector", (lambda i=i, t2=t2: lambda h: h.tensor_copy(
                    out=hbf[i], in_=t2))(),
                    reads=[rt2], writes=hrw)
            if fp32_dst is not None:
                p.dma("sync", (lambda i=i, t2=t2: lambda h: h.dma_start(
                    out=fp32_dst[i * 128:(i + 1) * 128, :], in_=t2))(),
                    reads=[rt2], writes=[f"{fp32_res}_{i}", fp32_res], sem_key="xs")
            if t2_cb is not None:
                t2_cb(i, t2, rt2)

    def stage1():
        load_vec("lane_f", EI["lane_f"], 1)
        load_vec("a_norm1_w", EI["a_norm1_w"], NT)
        load_vec("a_norm2_w", EI["a_norm2_w"], NT)
        load_vec("a_in_proj_b", EI["a_in_proj_b"], 2 * NTI)
        load_vec("a_in_proj_bh", EI["a_in_proj_bh"], NTI)
        load_vec("a_conv_w", EI["a_conv_w"], NTI * 3)
        load_vec("a_conv_b", EI["a_conv_b"], NTI)
        load_vec("a_dtbc_b1", EI["a_dtbc_b1"], 1)
        load_vec("a_dtbc_b2", EI["a_dtbc_b2"], 1)
        load_vec("a_D", EI["a_D"], NTI)
        load_vec("a_out_b", EI["a_out_b"], NT)
        load_vec("a_gate_w", EI["a_gate_w"], NT * NEXP)

        emit_norm(xdr, "xdr", "s1ln", "ln", fp32_dst=resdr, fp32_res="cur")
        dbg_dram("ln0", resdr, "cur")
        emit_norm(resdr, "cur", "s1r1", "rms", gamma=lambda i: vcol("a_norm1_w", i), eps=1e-6)
        hbr1 = [f"s1r1_hbf{i}" for i in range(NT)]
        dbg_sbuf("rms1", [hbf[i] for i in range(NT)], hbr1,
                 regions=["B0", "B1", "B2", "B3"])

        hx = f32v(X8[6])[:, 0:24]
        p.dma("sync", lambda h: h.dma_start(
            out=hx.rearrange("p (i c) -> p i c", i=8),
            in_=EI["x_halo"].rearrange("(i p) c -> p i c", p=128)),
            writes=["halo_x", "X6"], sem_key="xs")
        p.op("scalar", lambda h: h.activation(out=b16v(X8[7])[:, 0:24], in_=hx,
                                              func=AF.Square),
             reads=["halo_x"], writes=["halo_sq", "X7"])
        p.op("tensor", lambda h: h.matmul(out=PSQ[0][0:1, 0:24], lhsT=ONESB[:, :],
                                          rhs=b16v(X8[7])[:, 0:24], start=True, stop=True),
             reads=["halo_sq", "ONESB"], writes=[BK[0]])
        hrow = f32v(X8[2])[0:1, 0:3]
        p.op("vector", lambda h: h.tensor_copy(out=hrow, in_=PSQ[0][0:1, 0:3]),
             reads=[BK[0], "X2"], writes=["halo_a0", "X2"])
        for i in range(1, NT):
            p.op("vector", (lambda i=i: lambda h: h.tensor_add(
                out=hrow, in0=hrow, in1=PSQ[0][0:1, i * 3:i * 3 + 3]))(),
                reads=[BK[0], f"halo_a{i-1}"], writes=[f"halo_a{i}", "X2"])
        p.op("vector", lambda h: h.tensor_scalar(
            out=hrow, in0=hrow, scalar1=1.0 / D, scalar2=None, op0=ALU.mult),
            reads=[f"halo_a{NT-1}"], writes=["halo_ms", "X2"])
        rsqrt_inplace(hrow, 1e-6, "halo_inv", ["halo_ms"], "X2")
        bcast_row(hrow, f32v(X8[3])[0:128, 0:3], 128, "halo_invrep", ["halo_inv", "X2"],
                  width=3, dst_res="X3")
        for i in range(NT):
            p.op("vector", (lambda i=i: lambda h: h.scalar_tensor_tensor(
                out=b16v(X8[7])[:, 32 + i * 4:32 + i * 4 + 3],
                in0=f32v(X8[6])[:, i * 3:i * 3 + 3], scalar=vcol("a_norm1_w", i),
                in1=f32v(X8[3])[0:128, 0:3], op0=ALU.mult, op1=ALU.mult))(),
                reads=["halo_x", "halo_invrep", "gb_vecs"], writes=[f"halo_h{i}", "X7"])
        halo_hr = [f"halo_h{i}" for i in range(NT)]

        xm = [A8[o // 2].ap()[:, (o % 2) * 2052 + 4:(o % 2) * 2052 + 4 + 2048]
              for o in range(NTI)]
        xmp = [A8[o // 2].ap()[:, (o % 2) * 2052:(o % 2) * 2052 + 4] for o in range(NTI)]
        xmr = [ARES[o // 2] for o in range(NTI)]
        WSL = [B8[4].ap(), B8[5].ap(), B8[6].ap()]
        WRR = [BRES[4], BRES[5], BRES[6]]

        for g in range(8):
            slot = g % 3
            wr = f"w_ip{g}"
            wload(WSL[slot].rearrange("p (kc c) -> p kc c", kc=8),
                  EI["a_in_proj_w"].rearrange("(kc p) c -> p kc c", p=128)
                  [:, :, g * 512:(g + 1) * 512],
                  wr, dst_res=WRR[slot])
            for oi in range(4):
                o = g * 4 + oi
                bk = (o * 5) % 4

                for half in range(2):
                    def mmg2(h, oi=oi, slot=slot, bk=bk, half=half):
                        ins = []
                        for cc in range(2):
                            c = half * 2 + cc
                            for kc in range(NT):
                                ins.append(h.matmul(
                                    out=PSQ[bk][:, cc * 512:cc * 512 + 512],
                                    lhsT=WSL[slot][:, (kc * 4 + oi) * 128:(kc * 4 + oi) * 128 + 128],
                                    rhs=hbf[kc][:, c * 512:(c + 1) * 512],
                                    start=(kc == 0), stop=(kc == NT - 1)))
                        return ins
                    p.ops("tensor", mmg2, reads=[wr] + hbr1, writes=[BK[bk]])
                    for cc in range(2):
                        c = half * 2 + cc
                        if o < NTI:
                            p.op("scalar", (lambda o=o, c=c, cc=cc, bk=bk: lambda h: h.activation(
                                out=xm[o][:, c * 512:(c + 1) * 512],
                                in_=PSQ[bk][:, cc * 512:cc * 512 + 512],
                                func=AF.Identity, bias=vcol("a_in_proj_b", o)))(),
                                reads=[BK[bk], "gb_vecs"], writes=[f"xm{o}_{c}", xmr[o]])
                        else:
                            gt = o - NTI
                            p.op("scalar", (lambda c=c, cc=cc, bk=bk, o=o: lambda h: h.activation(
                                out=GS[c % 3][:, 0:512],
                                in_=PSQ[bk][:, cc * 512:cc * 512 + 512],
                                func=AF.Sigmoid, bias=vcol("a_in_proj_b", o)))(),
                                reads=[BK[bk], "gb_vecs"], writes=[f"sg{gt}_{c}", f"GS{c%3}"])
                            p.dma("sync", (lambda gt=gt, c=c: lambda h: h.dma_start(
                                out=spill[gt * 128:(gt + 1) * 128, c * 512:(c + 1) * 512],
                                in_=GS[c % 3][:, 0:512]))(),
                                reads=[f"sg{gt}_{c}"], writes=[f"sgd{gt}_{c}"], sem_key="sp")
                if o < NTI:
                    def mmh(h, oi=oi, slot=slot, bk=bk):
                        ins = []
                        for kc in range(NT):
                            ins.append(h.matmul(
                                out=PSQ[bk][:, 0:3],
                                lhsT=WSL[slot][:, (kc * 4 + oi) * 128:(kc * 4 + oi) * 128 + 128],
                                rhs=b16v(X8[7])[:, 32 + kc * 4:32 + kc * 4 + 3],
                                start=(kc == 0), stop=(kc == NT - 1)))
                        return ins
                    p.ops("tensor", mmh, reads=[wr] + halo_hr, writes=[BK[bk]])
                    p.op("scalar", (lambda o=o, bk=bk: lambda h: h.activation(
                        out=xmp[o][:, 1:4], in_=PSQ[bk][:, 0:3],
                        func=AF.Identity, bias=vcol("a_in_proj_bh", o)))(),
                        reads=[BK[bk], "gb_vecs"], writes=[f"xmp{o}", xmr[o]])

        accA = [b16v(X8[2])[:, 0:2048], b16v(X8[2])[:, 2048:4096]]
        for o in range(NTI):
            deps_o = [f"xm{o}_{c}" for c in range(NCH)] + [f"xmp{o}"]
            p.op("vector", (lambda o=o: lambda h: h.tensor_scalar(
                out=accA[0], in0=A8[o // 2].ap()[:, (o % 2) * 2052 + 2:(o % 2) * 2052 + 2 + 2048],
                scalar1=vcol("a_conv_w", o * 3 + 0), scalar2=None, op0=ALU.mult))(),
                reads=deps_o + ["gb_vecs"], writes=[f"cv0_{o}", "X2"])
            p.op("vector", (lambda o=o: lambda h: h.scalar_tensor_tensor(
                out=accA[1], in0=A8[o // 2].ap()[:, (o % 2) * 2052 + 3:(o % 2) * 2052 + 3 + 2048],
                scalar=vcol("a_conv_w", o * 3 + 1), in1=accA[0],
                op0=ALU.mult, op1=ALU.add))(),
                reads=deps_o + [f"cv0_{o}", "gb_vecs"], writes=[f"cv1_{o}", "X2"])
            p.op("vector", (lambda o=o: lambda h: h.scalar_tensor_tensor(
                out=accA[0], in0=xm[o],
                scalar=vcol("a_conv_w", o * 3 + 2), in1=accA[1],
                op0=ALU.mult, op1=ALU.add))(),
                reads=deps_o + [f"cv1_{o}", "gb_vecs"], writes=[f"cv2_{o}", "X2"])
            p.op("scalar", (lambda o=o: lambda h: h.activation(
                out=xm[o], in_=accA[0], func=AF.Silu, bias=vcol("a_conv_b", o)))(),
                reads=[f"cv2_{o}", "gb_vecs"], writes=[f"xs_{o}", xmr[o]])
        xsr1 = [f"xs_{o}" for o in range(NTI)]

        dbg_sbuf("xmain", [xm[o] for o in range(NTI)], [f"xs_{o}" for o in range(NTI)],
                 regions=["A0", "A1", "A2", "A3", "A4", "A5", "A6", "A7"])
        wload(WSL[0][:, 0:3072].rearrange("p (kc c) -> p kc c", kc=16),
              EI["a_dtbc_w"].rearrange("(kc p) c -> p kc c", p=128), "w_dtbc",
              dst_res=WRR[0])
        dt_ap = f32v(X8[5])[0:64, 0:2048]
        bm_ap = f32v(X8[6])[0:64, 0:2048]
        cm_ap = f32v(X8[7])[0:64, 0:2048]
        for c in range(NCH):
            bk = c % 2

            def mmdb(h, c=c, bk=bk):
                ins = []
                for kc in range(NTI):
                    ins.append(h.matmul(
                        out=PSQ[bk][:, 0:512],
                        lhsT=WSL[0][:, kc * 192:kc * 192 + 128],
                        rhs=xm[kc][:, c * 512:(c + 1) * 512],
                        start=(kc == 0), stop=(kc == NTI - 1)))
                for kc in range(NTI):
                    ins.append(h.matmul(
                        out=PSQ[bk][0:64, 512:1024],
                        lhsT=WSL[0][:, kc * 192 + 128:kc * 192 + 192],
                        rhs=xm[kc][:, c * 512:(c + 1) * 512],
                        start=(kc == 0), stop=(kc == NTI - 1)))
                return ins
            p.ops("tensor", mmdb, reads=["w_dtbc"] + xsr1, writes=[BK[bk]])
            p.op("scalar", (lambda c=c, bk=bk: lambda h: h.activation(
                out=dt_ap[:, c * 512:(c + 1) * 512], in_=PSQ[bk][0:64, 0:512],
                func=AF.Sigmoid, bias=VEC[0:64, vecmap["a_dtbc_b1"]:vecmap["a_dtbc_b1"] + 1]))(),
                reads=[BK[bk], "gb_vecs"], writes=[f"dt_{c}", "X5"])
            p.op("scalar", (lambda c=c, bk=bk: lambda h: h.activation(
                out=bm_ap[:, c * 512:(c + 1) * 512], in_=PSQ[bk][64:128, 0:512],
                func=AF.Identity, bias=VEC[64:128, vecmap["a_dtbc_b1"]:vecmap["a_dtbc_b1"] + 1]))(),
                reads=[BK[bk], "gb_vecs"], writes=[f"bm_{c}", "X6"])
            p.op("scalar", (lambda c=c, bk=bk: lambda h: h.activation(
                out=cm_ap[:, c * 512:(c + 1) * 512], in_=PSQ[bk][0:64, 512:1024],
                func=AF.Identity, bias=VEC[0:64, vecmap["a_dtbc_b2"]:vecmap["a_dtbc_b2"] + 1]))(),
                reads=[BK[bk], "gb_vecs"], writes=[f"cm_{c}", "X7"])
        dtr = [f"dt_{c}" for c in range(NCH)]
        bmr = [f"bm_{c}" for c in range(NCH)]
        cmr = [f"cm_{c}" for c in range(NCH)]

        a_ap = f32v(X8[2])[0:64, 0:2048]
        w_ap = f32v(X8[3])[0:64, 0:2048]
        s_ap = f32v(X8[0])[0:64, 0:2048]
        pP_ap = f32v(X8[1])[0:64, 0:2048]
        zz_ap = f32v(X8[4])[0:64, 0:2048]
        p.op("vector", lambda h: h.tensor_scalar(
            out=a_ap, in0=dt_ap, scalar1=-1.0, scalar2=1.0, op0=ALU.mult, op1=ALU.add),
            reads=dtr, writes=["sc_a", "X2"])
        p.op("vector", lambda h: h.tensor_mul(out=w_ap, in0=dt_ap, in1=bm_ap),
             reads=dtr + bmr, writes=["sc_w", "X3"])
        p.op("vector", lambda h: h.memset(zz_ap, 0.0), writes=["sc_zz", "X4"])
        zcol = valloc("zcol", 1)
        p.op("vector", lambda h: h.memset(zcol, 0.0), writes=["sc_zc"])
        onecol = valloc("onecol", 1)
        p.op("vector", lambda h: h.memset(onecol, 1.0), writes=["sc_oc"])
        p.op("vector", lambda h: h.tensor_tensor_scan(
            out=s_ap, data0=a_ap, data1=w_ap, initial=zcol[0:64, 0:1],
            op0=ALU.mult, op1=ALU.add),
            reads=["sc_a", "sc_w", "sc_zc"], writes=["sc_s0", "X0"])
        p.op("vector", lambda h: h.tensor_tensor_scan(
            out=pP_ap, data0=a_ap, data1=zz_ap, initial=onecol[0:64, 0:1],
            op0=ALU.mult, op1=ALU.add),
            reads=["sc_a", "sc_zz", "sc_oc"], writes=["sc_P", "X1"])
        so = xalloc(64)
        pair_write("a_state", lambda lane: [
            (xch[lane:lane + 1, so:so + 64], s_ap[:, 2047:2048])], ["sc_s0"])
        sin_col = valloc("a_sin", 1)
        pair_poll_read("a_state", lambda lane: [
            (sin_col[0:64, 0:1], xch[1 - lane:2 - lane, so:so + 64])],
            ["a_sin_raw"])
        p.op("vector", lambda h: h.tensor_mul(
            out=sin_col[0:64, 0:1], in0=sin_col[0:64, 0:1],
            in1=VEC[0:64, vecmap["lane_f"]:vecmap["lane_f"] + 1]),
            reads=["a_sin_raw", "gb_vecs"], writes=["a_sin_eff"])
        p.op("vector", lambda h: h.scalar_tensor_tensor(
            out=s_ap, in0=pP_ap, scalar=sin_col[0:64, 0:1], in1=s_ap,
            op0=ALU.mult, op1=ALU.add),
            reads=["sc_P", "a_sin_eff", "sc_s0"], writes=["sc_s", "X0"])
        y_ap = f32v(X8[6])[0:64, 0:2048]
        p.op("vector", lambda h: h.tensor_mul(out=y_ap, in0=cm_ap, in1=s_ap),
             reads=cmr + ["sc_s"], writes=["y_raw", "X6"])

        ybf = b16v(X8[2])[0:64, 0:2048]
        ysq = b16v(X8[2])[0:64, 2048:4096]
        p.op("vector", lambda h: h.tensor_copy(out=ybf, in_=y_ap),
             reads=["y_raw"], writes=["ybf", "X2"])
        p.op("scalar", lambda h: h.activation(out=ysq, in_=ybf, func=AF.Square),
             reads=["ybf"], writes=["ysq", "X2"])
        for c in range(NCH):
            p.op("tensor", (lambda c=c: lambda h: h.matmul(
                out=PSQ[c // 2][0:1, (c % 2) * 512:(c % 2) * 512 + 512],
                lhsT=ONESB[0:64, :], rhs=ybf[:, c * 512:(c + 1) * 512],
                start=True, stop=True))(),
                reads=["ybf", "ONESB"], writes=[BK[c // 2]])
            p.op("tensor", (lambda c=c: lambda h: h.matmul(
                out=PSQ[2 + c // 2][0:1, (c % 2) * 512:(c % 2) * 512 + 512],
                lhsT=ONESB[0:64, :], rhs=ysq[:, c * 512:(c + 1) * 512],
                start=True, stop=True))(),
                reads=["ysq", "ONESB"], writes=[BK[2 + c // 2]])
        ymu = f32v(X8[3])[0:1, 0:2048]
        yvr = f32v(X8[1])[0:1, 0:2048]
        ym2 = f32v(X8[5])[0:1, 0:2048]
        for c in range(NCH):
            p.op("vector", (lambda c=c: lambda h: h.tensor_scalar(
                out=ymu[0:1, c * 512:(c + 1) * 512],
                in0=PSQ[c // 2][0:1, (c % 2) * 512:(c % 2) * 512 + 512],
                scalar1=1.0 / DS_A, scalar2=None, op0=ALU.mult))(),
                reads=[BK[c // 2]], writes=[f"ymu{c}", "X3"])
            p.op("vector", (lambda c=c: lambda h: h.tensor_scalar(
                out=yvr[0:1, c * 512:(c + 1) * 512],
                in0=PSQ[2 + c // 2][0:1, (c % 2) * 512:(c % 2) * 512 + 512],
                scalar1=1.0 / DS_A, scalar2=None, op0=ALU.mult))(),
                reads=[BK[2 + c // 2]], writes=[f"yvr{c}", "X1"])
        ymud = [f"ymu{c}" for c in range(NCH)]
        yvrd = [f"yvr{c}" for c in range(NCH)]
        p.op("scalar", lambda h: h.activation(out=ym2, in_=ymu, func=AF.Square),
             reads=ymud, writes=["ym2", "X5"])
        p.op("vector", lambda h: h.tensor_sub(out=yvr, in0=yvr, in1=ym2),
             reads=["ym2"] + yvrd, writes=["yvar", "X1"])
        rsqrt_inplace(yvr, 1e-5, "yinv", ["yvar"], "X1")
        bcast_row(ymu, f32v(X8[7])[0:64, 0:2048], 64, "ymurep", ymud + ["X3"],
                  dst_res="X7")
        bcast_row(yvr, f32v(X8[5])[0:64, 0:2048], 64, "yinvrep", ["yinv", "X1"],
                  dst_res="X5")
        ymurep = f32v(X8[7])[0:64, 0:2048]
        yinvrep = f32v(X8[5])[0:64, 0:2048]
        ytmp = f32v(X8[1])[0:64, 0:2048]
        p.op("vector", lambda h: h.tensor_sub(out=ytmp, in0=y_ap, in1=ymurep),
             reads=["y_raw", "ymurep"], writes=["ysub", "X1"])
        yn_ext = b16v(X8[4])[0:65, 0:2048]
        p.op("vector", lambda h: h.tensor_mul(
            out=yn_ext[0:64, 0:2048], in0=ytmp, in1=yinvrep),
            reads=["ysub", "yinvrep"], writes=["yn", "X4"])
        p.op("vector", lambda h: h.memset(b16v(X8[4])[64:65, 0:2048], 1.0),
             writes=["yn_ones", "X4"])

        wload(WSL[1][0:65, 0:2048], EI["a_s2ib_w"], "w_s2ib", dst_res=WRR[1])
        y2lo = [b16v(X8[0])[:, o * 512:(o + 1) * 512] for o in range(8)]
        y2hi = [b16v(X8[1])[:, (o - 8) * 512:(o - 8) * 512 + 512] for o in range(8, 16)]

        def y2ap(o):
            return y2lo[o] if o < 8 else y2hi[o - 8]

        for c in range(NCH):
            p.dma("sync", (lambda c=c: lambda h: h.dma_start(
                out=b16v(X8[6]).rearrange("p (g c2) -> p g c2", g=8),
                in_=spill.rearrange("(g p) t -> p g t", p=128)
                [:, 0:8, c * 512:(c + 1) * 512]))(),
                reads=[f"sgd{gt}_{c}" for gt in range(8)], writes=[f"gch_lo{c}", "X6"],
                sem_key="xs")
            p.dma("sync", (lambda c=c: lambda h: h.dma_start(
                out=b16v(X8[2]).rearrange("p (g c2) -> p g c2", g=8),
                in_=spill.rearrange("(g p) t -> p g t", p=128)
                [:, 8:16, c * 512:(c + 1) * 512]))(),
                reads=[f"sgd{gt}_{c}" for gt in range(8, 16)], writes=[f"gch_hi{c}", "X2"],
                sem_key="xs")
            for o in range(NTI):
                bk = o % 4

                def mm1(h, o=o, c=c, bk=bk):
                    return [h.matmul(out=PSQ[bk][:, 0:512],
                                     lhsT=WSL[1][0:65, o * 128:(o + 1) * 128],
                                     rhs=yn_ext[:, c * 512:(c + 1) * 512],
                                     start=True, stop=True)]
                p.ops("tensor", mm1, reads=["w_s2ib", "yn", "yn_ones"], writes=[BK[bk]])
                p.op("vector", (lambda o=o, c=c, bk=bk: lambda h: h.scalar_tensor_tensor(
                    out=EVa[:, 0:512], in0=xm[o][:, c * 512:(c + 1) * 512],
                    scalar=vcol("a_D", o), in1=PSQ[bk][:, 0:512],
                    op0=ALU.mult, op1=ALU.add))(),
                    reads=[BK[bk], f"xs_{o}", "gb_vecs"], writes=[f"y2a{o}_{c}", "EVa"])
                gch = (b16v(X8[6]) if o < 8 else b16v(X8[2]))
                gsl = gch[:, (o % 8) * 512:(o % 8) * 512 + 512]
                p.op("vector", (lambda o=o, c=c, gsl=gsl: lambda h: h.tensor_mul(
                    out=y2ap(o), in0=EVa[:, 0:512], in1=gsl))(),
                    reads=[f"y2a{o}_{c}", f"gch_lo{c}" if o < 8 else f"gch_hi{c}"],
                    writes=[f"y2_{o}_{c}", "X0" if o < 8 else "X1"])
            for oo in range(NT):
                slot = [WSL[0], WSL[2]][oo % 2]
                slr = [WRR[0], WRR[2]][oo % 2]
                wr = f"w_ao{c}_{oo}"
                wload(slot[:, 0:2048].rearrange("p (kc c2) -> p kc c2", kc=16),
                      EI["a_out_wo"][oo].rearrange("kc p c2 -> p kc c2"), wr,
                      dst_res=slr)
                bk = oo % 4

                def mmo(h, oo=oo, c=c, slot=slot, bk=bk):
                    ins = []
                    for kc in range(NTI):
                        ins.append(h.matmul(
                            out=PSQ[bk][:, 512:1024],
                            lhsT=slot[:, kc * 128:(kc + 1) * 128],
                            rhs=y2ap(kc),
                            start=(kc == 0), stop=(kc == NTI - 1)))
                    return ins
                p.ops("tensor", mmo,
                      reads=[wr] + [f"y2_{kc}_{c}" for kc in range(NTI)],
                      writes=[BK[bk]])
                p.dma("sync", (lambda oo=oo, c=c: lambda h: h.dma_start(
                    out=EVb[:, 0:512],
                    in_=resdr[oo * 128:(oo + 1) * 128, c * 512:(c + 1) * 512]))(),
                    reads=["cur"], writes=[f"curc{oo}_{c}", "EVb"], sem_key="xs")
                p.op("vector", (lambda oo=oo, c=c, bk=bk: lambda h: h.scalar_tensor_tensor(
                    out=EVb[:, 0:512], in0=PSQ[bk][:, 512:1024],
                    scalar=vcol("a_out_b", oo), in1=EVb[:, 0:512],
                    op0=ALU.add, op1=ALU.add))(),
                    reads=[BK[bk], f"curc{oo}_{c}", "gb_vecs"],
                    writes=[f"curn{oo}_{c}", "EVb"])
                p.dma("sync", (lambda oo=oo, c=c: lambda h: h.dma_start(
                    out=resdr[oo * 128:(oo + 1) * 128, c * 512:(c + 1) * 512],
                    in_=EVb[:, 0:512]))(),
                    reads=[f"curn{oo}_{c}"], writes=["cur", f"cur2_{oo}_{c}"],
                    sem_key="xs")

        dbg_dram("mixer", resdr, "cur")
        dbg_sbuf("dtbc", [dt_ap, bm_ap, cm_ap], ["dt_3", "bm_3", "cm_3"], rows=64,
                 regions=["X5", "X6", "X7"])
        dbg_sbuf("scan", [s_ap, y_ap], ["sc_s", "y_raw"], rows=64,
                 regions=["X0", "X6"])
        gwf = valloc("gwf", NT * NEXP)
        for i in range(NT):
            p.op("vector", (lambda i=i: lambda h: h.tensor_scalar(
                out=vcols("gwf", i * 4, 4), in0=vcols("a_gate_w", i * 4, 4),
                scalar1=vcol("a_norm2_w", i), scalar2=None, op0=ALU.mult))(),
                reads=["gb_vecs"], writes=["gwf_r"])
        lacc = f32v(X8[0])[:, 0:64]

        def logits_cb(i, t2, rt2):
            def mml(h, i=i, t2=t2):
                ins = []
                for tt in range(16):
                    ins.append(h.matmul(
                        out=PSQ[0][:, tt * 4:tt * 4 + 4],
                        lhsT=t2[:, tt * 128:(tt + 1) * 128],
                        rhs=vcols("gwf", i * 4, 4),
                        start=(tt == 0), stop=(tt == 15)))
                return ins
            p.ops("tensor", mml, reads=[rt2, "gwf_r"], writes=[BK[0]])
            if i == 0:
                p.op("vector", lambda h: h.tensor_copy(out=lacc, in_=PSQ[0][:, 0:64]),
                     reads=[BK[0]], writes=["lacc", "X0"])
            else:
                p.op("vector", lambda h: h.tensor_add(out=lacc, in0=lacc,
                                                      in1=PSQ[0][:, 0:64]),
                     reads=[BK[0], "lacc"], writes=["lacc", "X0"])

        emit_norm(resdr, "cur", "s1r2", "rms", gamma=lambda i: vcol("a_norm2_w", i),
                  t2_cb=logits_cb, eps=1e-6)
        hbr2 = [f"s1r2_hbf{i}" for i in range(NT)]

        dbg_sbuf("rms2", [hbf[i] for i in range(NT)], hbr2,
                 regions=["B0", "B1", "B2", "B3"])
        gbrep = valloc("gbrep", 4)
        p.dma("gpsimd", lambda h: h.dma_start(
            out=gbrep, in_=EI["a_gate_b"].broadcast_to([128, NEXP])),
            writes=["gbrep_r"], sem_key="bc")
        lcols = valloc("l", 4)
        l2cols = valloc("l2", 4)
        m1c = valloc("m1", 1)
        nm1c = valloc("nm1", 1)
        m2c = valloc("m2", 1)
        ec = valloc("ec", 1)
        recc = valloc("recc", 1)
        w2c = valloc("w2c", 1)
        msk1 = valloc("msk1", 4)
        wtmp = valloc("wtmp", 4)
        wf = valloc("wf", 4)
        for tt in range(16):
            r = f"t2t{tt}"
            p.op("vector", (lambda tt=tt: lambda h: h.tensor_add(
                out=lcols, in0=lacc[:, tt * 4:tt * 4 + 4], in1=gbrep))(),
                reads=["lacc", "gbrep_r"], writes=[r + "l"])
            p.op("vector", lambda h: h.reduce_max(out=m1c, in_=lcols, axis=AX),
                 reads=[r + "l"], writes=[r + "m1"])
            p.op("vector", lambda h: h.tensor_scalar(
                out=msk1, in0=lcols, scalar1=m1c, scalar2=None, op0=ALU.is_ge),
                reads=[r + "l", r + "m1"], writes=[r + "k1"])
            p.op("vector", lambda h: h.scalar_tensor_tensor(
                out=l2cols, in0=msk1, scalar=-1e30, in1=lcols,
                op0=ALU.mult, op1=ALU.add),
                reads=[r + "k1", r + "l"], writes=[r + "l2"])
            p.op("vector", lambda h: h.reduce_max(out=m2c, in_=l2cols, axis=AX),
                 reads=[r + "l2"], writes=[r + "m2"])
            p.op("vector", lambda h: h.tensor_scalar(
                out=l2cols, in0=l2cols, scalar1=m2c, scalar2=None, op0=ALU.is_ge),
                reads=[r + "l2", r + "m2"], writes=[r + "k2"])
            p.op("vector", lambda h: h.tensor_scalar(
                out=nm1c, in0=m1c, scalar1=-1.0, scalar2=None, op0=ALU.mult),
                reads=[r + "m1"], writes=[r + "nm1"])
            p.op("scalar", lambda h: h.activation(
                out=ec, in_=m2c, func=AF.Exp, bias=nm1c),
                reads=[r + "m2", r + "nm1"], writes=[r + "e"])
            p.op("vector", lambda h: h.tensor_scalar(
                out=recc, in0=ec, scalar1=1.0, scalar2=None, op0=ALU.add),
                reads=[r + "e"], writes=[r + "den"])
            p.op("vector", lambda h: h.reciprocal(out=recc, in_=recc),
                 reads=[r + "den"], writes=[r + "rec"])
            p.op("vector", lambda h: h.tensor_mul(out=w2c, in0=ec, in1=recc),
                 reads=[r + "e", r + "rec"], writes=[r + "w2"])
            p.op("vector", lambda h: h.tensor_scalar(
                out=wtmp, in0=msk1, scalar1=recc, scalar2=None, op0=ALU.mult),
                reads=[r + "k1", r + "rec"], writes=[r + "wt"])
            p.op("vector", lambda h: h.scalar_tensor_tensor(
                out=wf, in0=l2cols, scalar=w2c, in1=wtmp,
                op0=ALU.mult, op1=ALU.add),
                reads=[r + "k2", r + "w2", r + "wt"], writes=[r + "wf"])
            qb = 2 + tt // 8

            def mtr(h, tt=tt, qb=qb):
                return [h.transpose(out=PSQ[qb][0:4, (tt % 8) * 128:(tt % 8) * 128 + 128],
                                    in_=wf, identity=IDNF[:, :])]
            p.ops("tensor", mtr, reads=[r + "wf", "IDNF"], writes=[BK[qb]])
        wfm = f32v(X8[6])[0:4, 0:2048]
        p.op("vector", lambda h: h.tensor_copy(out=wfm[0:4, 0:1024], in_=PSQ[2][0:4, 0:1024]),
             reads=[BK[2]], writes=["wfm_lo", "X6"])
        p.op("vector", lambda h: h.tensor_copy(out=wfm[0:4, 1024:2048], in_=PSQ[3][0:4, 0:1024]),
             reads=[BK[3]], writes=["wfm_hi", "X6"])
        dbg_sbuf("wfull", [wfm[0:4, 0:2048]], ["wfm_hi"], rows=4, regions=["X6", "wfm_lo"])
        for e in range(NEXP):
            p.dma("gpsimd", (lambda e=e: lambda h: h.dma_start(
                out=bc[4 + e:5 + e, 0:2048], in_=wfm[e:e + 1, 0:2048]))(),
                reads=["wfm_lo", "wfm_hi"], writes=[f"wrow{e}"], sem_key="bc")

        for oo in range(NT):
            p.dma("sync", (lambda oo=oo: lambda h: h.dma_start(
                out=f32v(X8[oo])[:, 0:2048], in_=resdr[oo * 128:(oo + 1) * 128, :]))(),
                reads=["cur"], writes=[f"acc{oo}", f"X{oo}"], sem_key="xs")

        eb1 = valloc("e_b1", 32)
        eb2 = valloc("e_b2", 8)
        for e in range(NEXP):
            p.dma("sync", (lambda e=e: lambda h: h.dma_start(
                out=vcols("e_b1", 0, 32), in_=EI["a_e_b1"][e]))(),
                writes=["eb1_r"], sem_key="wl2")
            p.dma("sync", (lambda e=e: lambda h: h.dma_start(
                out=vcols("e_b2", 0, 8), in_=EI["a_e_b2"][e]))(),
                writes=["eb2_r"], sem_key="wl2")
            p.dma("gpsimd", (lambda e=e: lambda h: h.dma_start(
                out=PBUF.ap()[:, 0:2048],
                in_=bc[4 + e:5 + e, 0:2048].broadcast_to([128, 2048])))(),
                reads=[f"wrow{e}"], writes=[f"wrep{e}", "PBUF"], sem_key="bc")
            for half in range(2):
                for otg in range(4):
                    slot = WSL[otg % 3]
                    slr = WRR[otg % 3]
                    wr = f"w_e1_{e}_{half}_{otg}"
                    wload(slot.rearrange("p (kc c) -> p kc c", kc=8),
                          EI["a_e_w1"][e].rearrange("(kc p) c -> p kc c", p=128)
                          [:, :, half * 2048 + otg * 512:half * 2048 + (otg + 1) * 512],
                          wr, dst_res=slr)
                    for oi in range(4):
                        ot = half * 16 + otg * 4 + oi
                        lt = otg * 4 + oi
                        bk = ot % 4

                        def mmw1(h, oi=oi, slot=slot, bk=bk):
                            ins = []
                            for cc in range(2):
                                for kc in range(NT):
                                    ins.append(h.matmul(
                                        out=PSQ[bk][:, cc * 512:cc * 512 + 512],
                                        lhsT=slot[:, (kc * 4 + oi) * 128:(kc * 4 + oi) * 128 + 128],
                                        rhs=hbf[kc][:, cc * 512:cc * 512 + 512],
                                        start=(kc == 0), stop=(kc == NT - 1)))
                            return ins
                        def mmw1b(h, oi=oi, slot=slot, bk=bk):
                            ins = []
                            for cc in range(2):
                                for kc in range(NT):
                                    ins.append(h.matmul(
                                        out=PSQ[bk][:, cc * 512:cc * 512 + 512],
                                        lhsT=slot[:, (kc * 4 + oi) * 128:(kc * 4 + oi) * 128 + 128],
                                        rhs=hbf[kc][:, 1024 + cc * 512:1024 + cc * 512 + 512],
                                        start=(kc == 0), stop=(kc == NT - 1)))
                            return ins
                        for gi, mf in ((0, mmw1), (1, mmw1b)):
                            p.ops("tensor", mf, reads=[wr] + hbr2, writes=[BK[bk]])
                            for cc in range(2):
                                c = gi * 2 + cc
                                p.op("scalar", (lambda ot=ot, lt=lt, c=c, cc=cc, bk=bk:
                                                lambda h: h.activation(
                                    out=A8[lt // 2].ap()[:, (lt % 2) * 2052 + c * 512:
                                                         (lt % 2) * 2052 + c * 512 + 512],
                                    in_=PSQ[bk][:, cc * 512:cc * 512 + 512],
                                    func=AF.Gelu, bias=VEC[:, vecmap["e_b1"] + ot:
                                                          vecmap["e_b1"] + ot + 1]))(),
                                    reads=[BK[bk], "eb1_r"],
                                    writes=[f"hid{lt}_{c}", ARES[lt // 2]])
                for oo in range(NT):
                    slot = WSL[oo % 3]
                    slr = WRR[oo % 3]
                    wr = f"w_e2_{e}_{half}_{oo}"
                    wload(slot[:, 0:2048].rearrange("p (kc c2) -> p kc c2", kc=16),
                          EI["a_e_w2o"][e, oo, half * 16:half * 16 + 16]
                          .rearrange("kc p c2 -> p kc c2"), wr, dst_res=slr)
                    for c in range(NCH):
                        bk = (oo + c) % 4

                        def mmw2(h, oo=oo, c=c, slot=slot, bk=bk):
                            ins = []
                            for kc in range(16):
                                ins.append(h.matmul(
                                    out=PSQ[bk][:, 0:512],
                                    lhsT=slot[:, kc * 128:(kc + 1) * 128],
                                    rhs=A8[kc // 2].ap()[:, (kc % 2) * 2052 + c * 512:
                                                         (kc % 2) * 2052 + c * 512 + 512],
                                    start=(kc == 0), stop=(kc == 15)))
                            return ins
                        p.ops("tensor", mmw2,
                              reads=[wr] + [f"hid{kc}_{c}" for kc in range(16)],
                              writes=[BK[bk]])
                        if half == 0:
                            p.op("scalar", (lambda bk=bk: lambda h: h.activation(
                                out=GS[0][:, 0:512], in_=PSQ[bk][:, 0:512],
                                func=AF.Identity))(),
                                reads=[BK[bk]], writes=["eo_h0", "GS0"])
                        else:
                            p.op("scalar", (lambda oo=oo, bk=bk: lambda h: h.activation(
                                out=GS[1][:, 0:512], in_=PSQ[bk][:, 0:512],
                                func=AF.Identity,
                                bias=VEC[:, vecmap["e_b2"] + oo:vecmap["e_b2"] + oo + 1]))(),
                                reads=[BK[bk], "eb2_r"], writes=["eo_h1", "GS1"])
                        src_gs = GS[0] if half == 0 else GS[1]
                        src_r = "eo_h0" if half == 0 else "eo_h1"
                        p.op("vector", (lambda c=c, src_gs=src_gs: lambda h:
                                        h.tensor_mul(
                            out=EVa[:, 0:512],
                            in0=src_gs[:, 0:512],
                            in1=PBUF.ap()[:, c * 512:(c + 1) * 512]))(),
                            reads=[src_r, f"wrep{e}"], writes=["eom", "EVa"])
                        p.op("vector", (lambda oo=oo, c=c: lambda h: h.tensor_add(
                            out=f32v(X8[oo])[:, c * 512:(c + 1) * 512],
                            in0=f32v(X8[oo])[:, c * 512:(c + 1) * 512],
                            in1=EVa[:, 0:512]))(),
                            reads=["eom", f"acc{oo}"], writes=[f"acc{oo}", f"X{oo}"])
            if e == 0:
                dbg_sbuf("acc_e0", [f32v(X8[oo])[:, 0:2048] for oo in range(NT)],
                         [f"acc{oo}" for oo in range(NT)])
            if e == 0:
                dbg_sbuf("hid_e0", [A8[j].ap()[:, 0:2048] for j in range(8)],
                         [f"hid{2*j}_3" for j in range(8)],
                         regions=[f"A{j}" for j in range(8)])

        dbg_sbuf("moeacc", [f32v(X8[oo])[:, 0:2048] for oo in range(NT)],
                 [f"acc{oo}" for oo in range(NT)])
        for oo in range(NT):
            for c in range(NCH):
                p.dma("sync", (lambda oo=oo, c=c: lambda h: h.dma_start(
                    out=EVb[:, 0:512],
                    in_=xdr[oo * 128:(oo + 1) * 128, c * 512:(c + 1) * 512]))(),
                    reads=["xdr"], writes=[f"xold{oo}_{c}", "EVb"], sem_key="xs")
                p.op("vector", (lambda oo=oo, c=c: lambda h: h.tensor_add(
                    out=EVb[:, 0:512], in0=EVb[:, 0:512],
                    in1=f32v(X8[oo])[:, c * 512:(c + 1) * 512]))(),
                    reads=[f"xold{oo}_{c}", f"acc{oo}"], writes=[f"xnew{oo}_{c}", "EVb"])
                p.dma("sync", (lambda oo=oo, c=c: lambda h: h.dma_start(
                    out=xdr[oo * 128:(oo + 1) * 128, c * 512:(c + 1) * 512],
                    in_=EVb[:, 0:512]))(),
                    reads=[f"xnew{oo}_{c}"], writes=["xdr", f"xup{oo}_{c}"], sem_key="xs")


    qsp = spill[2048:3072, :]

    def qsb(dc):
        return GS[dc // 4][:, (dc % 4) * 128:(dc % 4) * 128 + 128]

    def emit_ffn(cid):
        tg = f"ffn{cid}"
        if cid == 0:
            load_vec("f_node_b", EI["f_node_b"], 1)
            load_vec("f_negthr", EI["f_negthr"], 1)
            for f in ("q", "k", "v"):
                load_vec(f"f_{f}_b1", EI[f"f_{f}_b1"], 8)
            p.dma("gpsimd", lambda h: h.dma_start(out=SELW[:, :], in_=EI["f_selmat"]),
                  writes=["SELW"], sem_key="wl")
            p.dma("sync", lambda h: h.dma_start(
                out=NWS[:, 0:360].rearrange("p (i c) -> p i c", i=8),
                in_=EI["f_node_w"].rearrange("(i p) c -> p i c", p=128)),
                writes=["NWS"], sem_key="wl2")
        inv32 = const_col(1.0 / 32.0)
        ninv32 = const_col(-1.0 / 32.0)
        kvw = 1024 * 4096
        ko = xalloc(kvw // 2)
        vo = xalloc(kvw // 2)
        kslab = xchb[0:1, 2 * ko:2 * ko + kvw].rearrange("a (d t) -> (a d) t", d=1024)
        vtslab = xchb[0:1, 2 * vo:2 * vo + kvw].rearrange("a (t d) -> (a t) d", t=4096)

        def score_cb(i, t2, rt2):
            def mms(h, i=i, t2=t2):
                ins = []
                for c in range(NCH):
                    ins.append(h.matmul(
                        out=PSQ[c // 2][0:45, (c % 2) * 512:(c % 2) * 512 + 512],
                        lhsT=NWS[:, i * 45:(i + 1) * 45],
                        rhs=t2[:, c * 512:(c + 1) * 512],
                        start=(i == 0), stop=(i == NT - 1)))
                return ins
            p.ops("tensor", mms, reads=[rt2, "NWS"], writes=[BK[0], BK[1]])
        dbg_dram(f"{tg}x" if cid == 0 else "zzx", xdr, "xdr")
        emit_norm(xdr, "xdr", f"{tg}ln", "ln", t2_cb=score_cb)
        hbrf = [f"{tg}ln_hbf{i}" for i in range(NT)]

        g_sb = b16v(X8[0])[0:45, 0:2048]
        for c in range(NCH):
            p.op("vector", (lambda c=c: lambda h: h.tensor_scalar(
                out=g_sb[:, c * 512:(c + 1) * 512],
                in0=PSQ[c // 2][0:45, (c % 2) * 512:(c % 2) * 512 + 512],
                scalar1=VEC[0:45, vecmap["f_node_b"]:vecmap["f_node_b"] + 1],
                scalar2=0.0, op0=ALU.add, op1=ALU.is_ge))(),
                reads=[BK[c // 2], "gb_vecs"], writes=[f"{tg}_g{c}", "X0"])
        for c in range(NCH):
            p.op("tensor", (lambda c=c: lambda h: h.matmul(
                out=PSQ[2 + c // 2][0:96, (c % 2) * 512:(c % 2) * 512 + 512],
                lhsT=SELW[:, :], rhs=g_sb[:, c * 512:(c + 1) * 512],
                start=True, stop=True))(),
                reads=[f"{tg}_g{c}", "SELW"], writes=[BK[2 + c // 2]])
        sel_sb = b16v(X8[1])[0:96, 0:2048]
        for c in range(NCH):
            p.op("vector", (lambda c=c: lambda h: h.tensor_scalar(
                out=sel_sb[:, c * 512:(c + 1) * 512],
                in0=PSQ[2 + c // 2][0:96, (c % 2) * 512:(c % 2) * 512 + 512],
                scalar1=VEC[0:96, vecmap["f_negthr"]:vecmap["f_negthr"] + 1],
                scalar2=0.0, op0=ALU.add, op1=ALU.is_ge))(),
                reads=[BK[2 + c // 2], "gb_vecs"], writes=[f"{tg}_sel{c}", "X1"])
        selrd = [f"{tg}_sel{c}" for c in range(NCH)]
        p.dma("sync", lambda h: h.dma_start(out=selr, in_=sel_sb),
              reads=selrd, writes=[f"{tg}_selr"], sem_key="sp")
        dbg_sbuf(f"{tg}g" if cid == 0 else "zzg", [g_sb], [f"{tg}_g3"], rows=45,
                 regions=["X0"] + [f"{tg}_g{c}" for c in range(3)])
        dbg_sbuf(f"{tg}sel" if cid == 0 else "zzs", [sel_sb], [f"{tg}_sel3"], rows=96,
                 regions=["X1"] + selrd[:3])

        def wtile(kc, ot):
            t = B8[6].ap() if kc < 4 else PBUF.ap()
            return t[:, ((kc % 4) * 8 + ot) * 128:((kc % 4) * 8 + ot) * 128 + 128]

        def kvloc(ot):
            return b16v(X8[4 + ot // 2])[:, (ot % 2) * 2048:(ot % 2) * 2048 + 2048]

        def vfm(dt_):
            if dt_ < 4:
                return B8[4 + dt_ // 2].ap()[:, (dt_ % 2) * 2048:(dt_ % 2) * 2048 + 2048]
            return b16v(X8[2 + (dt_ - 4) // 2])[:, ((dt_ - 4) % 2) * 2048:
                                                ((dt_ - 4) % 2) * 2048 + 2048]

        def vfm_res(dt_):
            return BRES[4 + dt_ // 2] if dt_ < 4 else f"X{2 + (dt_ - 4) // 2}"

        for f in ("k", "v", "q"):
            fi = {"q": 0, "k": 1, "v": 2}[f]
            ftg = f"{tg}{f}"
            for half, dstt, dres in ((0, B8[6].ap(), "B6"), (1, PBUF.ap(), "PBUF")):
                p.dma("gpsimd", (lambda half=half, dstt=dstt, f=f: lambda h: h.dma_start(
                    out=dstt.rearrange("p (kc c) -> p kc c", kc=4),
                    in_=EI[f"f_{f}_w1"].rearrange("(kc p) c -> p kc c", p=128)
                    [:, half * 4:half * 4 + 4, :]))(),
                    writes=[f"{ftg}_w1{half}", dres], sem_key="wl")
            w1r = [f"{ftg}_w10", f"{ftg}_w11"]
            for ot in range(NT):
                bk = ot % 4
                for c in range(NCH):
                    def mmh(h, ot=ot, c=c, bk=bk):
                        ins = []
                        for kc in range(NT):
                            ins.append(h.matmul(
                                out=PSQ[bk][:, (c % 2) * 512:(c % 2) * 512 + 512],
                                lhsT=wtile(kc, ot),
                                rhs=hbf[kc][:, c * 512:(c + 1) * 512],
                                start=(kc == 0), stop=(kc == NT - 1)))
                        return ins
                    p.ops("tensor", mmh, reads=w1r + hbrf, writes=[BK[bk]])
                    p.op("scalar", (lambda ot=ot, c=c, bk=bk, f=f: lambda h: h.activation(
                        out=A8[ot // 2].ap()[:, (ot % 2) * 2052 + c * 512:
                                             (ot % 2) * 2052 + c * 512 + 512],
                        in_=PSQ[bk][:, (c % 2) * 512:(c % 2) * 512 + 512],
                        func=AF.Relu, bias=vcol(f"f_{f}_b1", ot)))(),
                        reads=[BK[bk], "gb_vecs"],
                        writes=[f"{ftg}_hid{ot}_{c}", ARES[ot // 2]])
                sr = b16v(X8[2 + ot % 2])[:, 0:2048]
                p.dma("gpsimd", (lambda ot=ot, fi=fi, sr=sr: lambda h: h.dma_start(
                    out=sr[0:64, :],
                    in_=selr[fi * 32 + 2 * ot:fi * 32 + 2 * ot + 1, :]
                    .broadcast_to([64, TLOC])))(),
                    reads=[f"{tg}_selr"],
                    writes=[f"{ftg}_srA{ot}", f"X{2 + ot % 2}"], sem_key="bc")
                p.dma("gpsimd", (lambda ot=ot, fi=fi, sr=sr: lambda h: h.dma_start(
                    out=sr[64:128, :],
                    in_=selr[fi * 32 + 2 * ot + 1:fi * 32 + 2 * ot + 2, :]
                    .broadcast_to([64, TLOC])))(),
                    reads=[f"{tg}_selr"],
                    writes=[f"{ftg}_srB{ot}", f"X{2 + ot % 2}"], sem_key="bc")
                p.op("vector", (lambda ot=ot, sr=sr: lambda h: h.tensor_mul(
                    out=A8[ot // 2].ap()[:, (ot % 2) * 2052:(ot % 2) * 2052 + 2048],
                    in0=A8[ot // 2].ap()[:, (ot % 2) * 2052:(ot % 2) * 2052 + 2048],
                    in1=sr))(),
                    reads=[f"{ftg}_hid{ot}_{c2}" for c2 in range(NCH)] +
                          [f"{ftg}_srA{ot}", f"{ftg}_srB{ot}"],
                    writes=[f"{ftg}_hm{ot}", ARES[ot // 2]])
            hmr = [f"{ftg}_hm{kc}" for kc in range(NT)]
            for half, dstt, dres in ((0, B8[6].ap(), "B6"), (1, PBUF.ap(), "PBUF")):
                p.dma("gpsimd", (lambda half=half, dstt=dstt, f=f: lambda h: h.dma_start(
                    out=dstt.rearrange("p (kc c) -> p kc c", kc=4),
                    in_=EI[f"f_{f}_w2"].rearrange("(kc p) c -> p kc c", p=128)
                    [:, half * 4:half * 4 + 4, :]))(),
                    reads=hmr, writes=[f"{ftg}_w2{half}", dres], sem_key="wl")
            p.dma("gpsimd", (lambda f=f, fi=fi: lambda h: h.dma_start(
                out=B2S[fi * 32:fi * 32 + 16, :], in_=EI[f"f_{f}_b2"]))(),
                writes=[f"{ftg}_b2", "B2S"], sem_key="wl")
            w2r = [f"{ftg}_w20", f"{ftg}_w21", f"{ftg}_b2"]
            for ot in range(NT):
                bk = (ot + 2) % 4
                for c in range(NCH):
                    def mmw2(h, ot=ot, c=c, bk=bk, fi=fi):
                        ins = []
                        for kc in range(NT):
                            ins.append(h.matmul(
                                out=PSQ[bk][:, (c % 2) * 512:(c % 2) * 512 + 512],
                                lhsT=wtile(kc, ot),
                                rhs=A8[kc // 2].ap()[:, (kc % 2) * 2052 + c * 512:
                                                     (kc % 2) * 2052 + c * 512 + 512],
                                start=(kc == 0), stop=False))
                        ins.append(h.matmul(
                            out=PSQ[bk][:, (c % 2) * 512:(c % 2) * 512 + 512],
                            lhsT=B2S[fi * 32:fi * 32 + 16, ot * 128:(ot + 1) * 128],
                            rhs=sel_sb[fi * 32:fi * 32 + 16, c * 512:(c + 1) * 512],
                            start=False, stop=True))
                        return ins
                    p.ops("tensor", mmw2, reads=w2r + hmr + selrd, writes=[BK[bk]])
                    dst = vfm(ot) if f == "v" else kvloc(ot)
                    dres2 = vfm_res(ot) if f == "v" else f"X{4 + ot // 2}"
                    p.op("scalar", (lambda ot=ot, c=c, bk=bk, dst=dst: lambda h:
                                    h.activation(
                        out=dst[:, c * 512:(c + 1) * 512],
                        in_=PSQ[bk][:, (c % 2) * 512:(c % 2) * 512 + 512],
                        func=AF.Identity))(),
                        reads=[BK[bk]], writes=[f"{ftg}_o{ot}_{c}", dres2])
            odeps = [f"{ftg}_o{ot}_{c}" for ot in range(NT) for c in range(NCH)]
            if f == "k":
                pair_write(f"{tg}k", lambda l: [
                    (kslab[ot * 128:(ot + 1) * 128, l * 2048:(l + 1) * 2048], kvloc(ot))
                    for ot in range(NT)], odeps)
                dbg_sbuf(f"{tg}kv" if cid == 0 else "zzk",
                         [kvloc(ot) for ot in range(NT)],
                         [f"{ftg}_o{ot}_3" for ot in range(NT)],
                         regions=[f"X{4 + j}" for j in range(4)] + odeps)
            elif f == "v":
                for dt_ in range(8):
                    for tc in range(16):
                        s = (dt_ * 16 + tc) % 2

                        def ttr(h, dt_=dt_, tc=tc, s=s):
                            return [h.transpose(
                                out=PSQ[s].ap().bitcast(BF16)[0:128, 0:128],
                                in_=vfm(dt_)[:, tc * 128:(tc + 1) * 128],
                                identity=IDN[:, :])]
                        p.ops("tensor", ttr,
                              reads=[f"{ftg}_o{dt_}_{tc // 4}", "IDN"], writes=[BK[s]])
                        p.op("vector", (lambda dt_=dt_, tc=tc, s=s: lambda h:
                                        h.tensor_copy(
                            out=b16v(X8[4 + tc // 4])[:, (tc % 4) * 1024 + dt_ * 128:
                                                      (tc % 4) * 1024 + dt_ * 128 + 128],
                            in_=PSQ[s].ap().bitcast(BF16)[0:128, 0:128]))(),
                            reads=[BK[s]],
                            writes=[f"{tg}_vt{dt_}_{tc}", f"X{4 + tc // 4}"])
                vt_tags = [f"{tg}_vt{dt_}_{tc}" for dt_ in range(8) for tc in range(16)]
                pair_write(f"{tg}v", lambda l: [
                    (vtslab[l * 2048 + j * 512:l * 2048 + (j + 1) * 512, :]
                     .rearrange("(jj p) d -> p jj d", p=128),
                     b16v(X8[4 + j]).rearrange("p (jj d) -> p jj d", jj=4))
                    for j in range(4)], vt_tags)
                dbg_sbuf(f"{tg}vv" if cid == 0 else "zzv",
                         [vfm(ot) for ot in range(NT)],
                         [f"{ftg}_o{ot}_3" for ot in range(NT)],
                         regions=["B4", "B5", "X2", "X3"] + odeps)
            else:
                for j in range(4):
                    p.dma("sync", (lambda j=j: lambda h: h.dma_start(
                        out=qsp[j * 256:(j + 1) * 256, :]
                        .rearrange("(o p) t -> p o t", p=128),
                        in_=b16v(X8[4 + j]).rearrange("p (o t) -> p o t", o=2)))(),
                        reads=[f"{ftg}_o{2 * j}_{c}" for c in range(NCH)] +
                              [f"{ftg}_o{2 * j + 1}_{c}" for c in range(NCH)],
                        writes=[f"{tg}_qsp{j}", "qspill"], sem_key="sp")
                dbg_sbuf(f"{tg}qv" if cid == 0 else "zzq",
                         [kvloc(ot) for ot in range(NT)],
                         [f"{ftg}_o{ot}_3" for ot in range(NT)],
                         regions=[f"X{4 + j}" for j in range(4)] + odeps)

        pair_poll_read(f"{tg}k", lambda l: [
            (A8[i].ap()[:, 0:4096], kslab[i * 128:(i + 1) * 128, :])
            for i in range(NT)],
            [f"{tg}_K"] + [ARES[i] for i in range(NT)], extra_deps=[f"xw_{tg}k"])
        pair_poll_read(f"{tg}v", lambda l: [
            (b16v(X8[i]).rearrange("p (jj d) -> p jj d", jj=4),
             vtslab[i * 512:(i + 1) * 512, :].rearrange("(jj p) d -> p jj d", p=128))
            for i in range(NT)],
            [f"{tg}_VT"] + [f"X{i}" for i in range(NT)], extra_deps=[f"xw_{tg}v"])

        dbg_sbuf(f"{tg}K0" if cid == 0 else "zzK0",
                 [A8[i].ap()[:, 0:2048] for i in range(NT)],
                 [f"{tg}_K"] * NT, regions=[ARES[i] for i in range(NT)])
        dbg_sbuf(f"{tg}K1" if cid == 0 else "zzK1",
                 [A8[i].ap()[:, 2048:4096] for i in range(NT)],
                 [f"{tg}_K"] * NT, regions=[ARES[i] for i in range(NT)])
        dbg_sbuf(f"{tg}VT0" if cid == 0 else "zzVT0",
                 [b16v(X8[i])[:, 0:2048] for i in range(NT)],
                 [f"{tg}_VT"] * NT, regions=[f"X{i}" for i in range(NT)])
        dbg_sbuf(f"{tg}VT1" if cid == 0 else "zzVT1",
                 [b16v(X8[i])[:, 2048:4096] for i in range(NT)],
                 [f"{tg}_VT"] * NT, regions=[f"X{i}" for i in range(NT)])

        PTS = [B8[j] for j in range(4)]
        for qb in range(4):
            for qt4 in range(4):
                qt = qb * 4 + qt4
                for j in range(2):
                    p.dma("sync", (lambda qt=qt, j=j: lambda h: h.dma_start(
                        out=GS[j].ap().rearrange("p (dc c) -> p dc c", dc=4),
                        in_=qsp.rearrange("(dc p) t -> p dc t", p=128)
                        [:, j * 4:(j + 1) * 4, qt * 128:(qt + 1) * 128]))(),
                        reads=[f"{tg}_qsp{jj}" for jj in range(4)],
                        writes=[f"{tg}_qs{qt}", f"GS{j}"], sem_key="xs")
                for dc in range(8):
                    def mmsc(h, dc=dc):
                        ins = []
                        for kb in range(8):
                            ins.append(h.matmul(
                                out=PSQ[kb // 2][:, (kb % 2) * 512:(kb % 2) * 512 + 512],
                                lhsT=qsb(dc),
                                rhs=A8[dc].ap()[:, kb * 512:(kb + 1) * 512],
                                start=(dc == 0), stop=(dc == 7)))
                        return ins
                    p.ops("tensor", mmsc,
                          reads=[f"{tg}_qs{qt}", f"{tg}_K", f"GS{dc // 4}", ARES[dc]],
                          writes=[BK[0], BK[1], BK[2], BK[3]])
                for kb in range(8):
                    p.op("vector", (lambda kb=kb: lambda h: h.reduce_max(
                        out=EVa[:, kb:kb + 1],
                        in_=PSQ[kb // 2][:, (kb % 2) * 512:(kb % 2) * 512 + 512],
                        axis=AX))(),
                        reads=[BK[kb // 2]], writes=[f"{tg}_mx{kb}", "EVa"])
                p.op("vector", lambda h: h.reduce_max(
                    out=EVa[:, 8:9], in_=EVa[:, 0:8], axis=AX),
                    reads=[f"{tg}_mx{kb2}" for kb2 in range(8)],
                    writes=[f"{tg}_m", "EVa"])
                p.op("vector", lambda h: h.tensor_scalar(
                    out=EVa[:, 9:10], in0=EVa[:, 8:9], scalar1=ninv32[0:128, 0:1],
                    scalar2=None, op0=ALU.mult),
                    reads=[f"{tg}_m", "gb_vecs"], writes=[f"{tg}_b32", "EVa"])
                for kb in range(8):
                    p.op("scalar", (lambda kb=kb: lambda h: h.activation(
                        out=PBUF.ap()[:, kb * 512:(kb + 1) * 512],
                        in_=PSQ[kb // 2][:, (kb % 2) * 512:(kb % 2) * 512 + 512],
                        func=AF.Exp, scale=inv32[0:128, 0:1], bias=EVa[:, 9:10],
                        accum_out=EVa[:, 16 + kb:17 + kb]))(),
                        reads=[BK[kb // 2], f"{tg}_b32", "gb_vecs"],
                        writes=[f"{tg}_e{kb}", "PBUF", "EVa"])
                erd = [f"{tg}_e{kb2}" for kb2 in range(8)]
                p.op("vector", lambda h: h.tensor_add(
                    out=EVa[:, 24:28], in0=EVa[:, 16:20], in1=EVa[:, 20:24]),
                    reads=erd, writes=[f"{tg}_s1", "EVa"])
                p.op("vector", lambda h: h.tensor_add(
                    out=EVa[:, 28:30], in0=EVa[:, 24:26], in1=EVa[:, 26:28]),
                    reads=[f"{tg}_s1"], writes=[f"{tg}_s2", "EVa"])
                p.op("vector", lambda h: h.tensor_add(
                    out=EVa[:, 30:31], in0=EVa[:, 28:29], in1=EVa[:, 29:30]),
                    reads=[f"{tg}_s2"], writes=[f"{tg}_s3", "EVa"])
                p.op("vector", lambda h: h.reciprocal(
                    out=EVa[:, 31:32], in_=EVa[:, 30:31]),
                    reads=[f"{tg}_s3"], writes=[f"{tg}_rec", "EVa"])
                for hh in range(2):
                    p.op("vector", (lambda hh=hh: lambda h: h.tensor_scalar(
                        out=PBUF.ap()[:, hh * 2048:(hh + 1) * 2048],
                        in0=PBUF.ap()[:, hh * 2048:(hh + 1) * 2048],
                        scalar1=EVa[:, 31:32], scalar2=None, op0=ALU.mult))(),
                        reads=[f"{tg}_rec"] + erd, writes=[f"{tg}_pn{hh}", "PBUF"])
                if cid == 0 and qt == 15:
                    dbg_sbuf(f"{tg}P", [PBUF.ap()[:, 0:2048]], [f"{tg}_pn0"],
                             regions=["PBUF", f"{tg}_pn1"])
                for kt in range(32):
                    s = kt % 2

                    def ptr(h, kt=kt, s=s):
                        return [h.transpose(
                            out=PSQ[s].ap().bitcast(BF16)[0:128, 0:128],
                            in_=PBUF.ap()[:, kt * 128:(kt + 1) * 128],
                            identity=IDN[:, :])]
                    p.ops("tensor", ptr, reads=[f"{tg}_pn{kt // 16}", "IDN"],
                          writes=[BK[s]])
                    p.op("vector", (lambda kt=kt, s=s, qt4=qt4: lambda h: h.tensor_copy(
                        out=PTS[kt // 8].ap()[:, (kt % 8) * 512 + qt4 * 128:
                                              (kt % 8) * 512 + qt4 * 128 + 128],
                        in_=PSQ[s].ap().bitcast(BF16)[0:128, 0:128]))(),
                        reads=[BK[s]], writes=[f"{tg}_pt{qt4}_{kt}", BRES[kt // 8]])
            if cid == 0 and qb == 3:
                dbg_sbuf(f"{tg}PT0", [PTS[j].ap()[:, 0:2048] for j in range(4)],
                         [f"{tg}_pt3_{8 * j + 3}" for j in range(4)],
                         regions=[BRES[j] for j in range(4)] +
                         [f"{tg}_pt{qq}_{kt}" for qq in range(4) for kt in range(32)])
                dbg_sbuf(f"{tg}PT1", [PTS[j].ap()[:, 2048:4096] for j in range(4)],
                         [f"{tg}_pt3_{8 * j + 7}" for j in range(4)],
                         regions=[BRES[j] for j in range(4)])
            ptr_tags = [f"{tg}_pt{qq}_{kt}" for qq in range(4) for kt in range(32)]
            for dc in range(8):
                def mmav(h, dc=dc):
                    ins = []
                    for kt in range(32):
                        ins.append(h.matmul(
                            out=PSQ[dc // 2][:, (dc % 2) * 512:(dc % 2) * 512 + 512],
                            lhsT=b16v(X8[kt // 4])[:, (kt % 4) * 1024 + dc * 128:
                                                   (kt % 4) * 1024 + dc * 128 + 128],
                            rhs=PTS[kt // 8].ap()[:, (kt % 8) * 512:(kt % 8) * 512 + 512],
                            start=(kt == 0), stop=(kt == 31)))
                    return ins
                p.ops("tensor", mmav,
                      reads=[f"{tg}_VT"] + ptr_tags + [f"X{ii}" for ii in range(8)],
                      writes=[BK[dc // 2]])
            if cid == 0 and qb == 3 and "ffn0AV" in DEBUG:
                p.op("vector", lambda h: h.tensor_copy(
                    out=EVb[:, 0:512], in_=PSQ[0][:, 0:512]),
                    reads=[BK[0]], writes=["avdbg", "EVb"])
                o_av = nc.dram_tensor("dbgav", [128, 512], F32,
                                      kind="ExternalOutput").ap()
                DBG_SHAPES["ffn0AV"] = "dbgav"
                p.dma("gpsimd", lambda h: h.dma_start(out=o_av, in_=EVb[:, 0:512]),
                      reads=["avdbg"], writes=["avdbgd"], sem_key="dbg")
            for dc in range(8):
                ev = EVb if dc % 2 == 0 else EVa
                evr = "EVb" if dc % 2 == 0 else "EVa"
                p.dma("sync", (lambda dc=dc, qb=qb, ev=ev: lambda h: h.dma_start(
                    out=ev[:, 0:512],
                    in_=xdr[dc * 128:(dc + 1) * 128, qb * 512:(qb + 1) * 512]))(),
                    reads=["xdr"], writes=[f"{tg}_xo{qb}_{dc}", evr], sem_key="xs")
                p.op("vector", (lambda dc=dc, ev=ev: lambda h: h.tensor_add(
                    out=ev[:, 0:512], in0=ev[:, 0:512],
                    in1=PSQ[dc // 2][:, (dc % 2) * 512:(dc % 2) * 512 + 512]))(),
                    reads=[f"{tg}_xo{qb}_{dc}", BK[dc // 2]],
                    writes=[f"{tg}_xn{qb}_{dc}", evr])
                p.dma("sync", (lambda dc=dc, qb=qb, ev=ev: lambda h: h.dma_start(
                    out=xdr[dc * 128:(dc + 1) * 128, qb * 512:(qb + 1) * 512],
                    in_=ev[:, 0:512]))(),
                    reads=[f"{tg}_xn{qb}_{dc}"], writes=["xdr", f"{tg}_xu{qb}_{dc}"],
                    sem_key="xs")
                if cid == 0 and qb == 3 and dc == 0 and "ffn0EV" in DEBUG:
                    o_ev = nc.dram_tensor("dbgev", [128, 512], F32,
                                          kind="ExternalOutput").ap()
                    DBG_SHAPES["ffn0EV"] = "dbgev"
                    p.dma("gpsimd", (lambda ev=ev: lambda h: h.dma_start(
                        out=o_ev, in_=ev[:, 0:512]))(),
                        reads=[f"{tg}_xn{qb}_{dc}"], writes=["evdbgd"], sem_key="dbg")
                    o_xd = nc.dram_tensor("dbgxd", [128, 512], F32,
                                          kind="ExternalOutput").ap()
                    DBG_SHAPES["ffn0XD"] = "dbgxd"
                    p.dma("gpsimd", (lambda qb=qb, dc=dc: lambda h: h.dma_start(
                        out=o_xd, in_=xdr[dc * 128:(dc + 1) * 128,
                                          qb * 512:(qb + 1) * 512]))(),
                        reads=[f"{tg}_xu{qb}_{dc}", "xdr"], writes=["xddbgd"],
                        sem_key="dbg")

    def stage3():
        bnw = valloc("b_nw", NT)
        bnb = valloc("b_nb", NT)
        bcb = valloc("b_cb", NTI)
        bdtb = valloc("b_dtb", NTI)
        bDc = valloc("b_D", NTI)
        zcol = const_col(0.0)
        onec = const_col(1.0)
        xc_sp = spill[0:2048, :]
        z_sp = spill[3072:5120, :]
        dts_o = xalloc(2048 * 4096 // 2)
        dtus_o = xalloc(2048 * 4096 // 2)
        ys_o = xalloc(2048 * 4096 // 2)
        bcs_o = xalloc(32 * 4096 // 2)
        hls_o = xalloc(2048 * 4 // 2)
        dt_sl = xchb[0:1, 2 * dts_o:2 * dts_o + 2048 * 4096]\
            .rearrange("a (d t) -> (a d) t", d=2048)
        dtu_sl = xchb[0:1, 2 * dtus_o:2 * dtus_o + 2048 * 4096]\
            .rearrange("a (d t) -> (a d) t", d=2048)
        y_sl = xchb[0:1, 2 * ys_o:2 * ys_o + 2048 * 4096]\
            .rearrange("a (d t) -> (a d) t", d=2048)
        bc_sl = xchb[0:1, 2 * bcs_o:2 * bcs_o + 32 * 4096]\
            .rearrange("a (d t) -> (a d) t", d=32)
        hl_sl = xchb[0:1, 2 * hls_o:2 * hls_o + 2048 * 4]\
            .rearrange("a (d t) -> (a d) t", d=2048)

        emit_norm(xdr, "xdr", "mbl0", "ln", fp32_dst=resdr, fp32_res="cur")
        dbg_dram("mbln0", resdr, "cur")

        xm = [A8[o // 2].ap()[:, (o % 2) * 2052 + 4:(o % 2) * 2052 + 4 + 2048]
              for o in range(NTI)]
        xmp = [A8[o // 2].ap()[:, (o % 2) * 2052:(o % 2) * 2052 + 4]
               for o in range(NTI)]
        ym = [A8[o // 2].ap()[:, (o % 2) * 2052:(o % 2) * 2052 + 2048]
              for o in range(NTI)]
        WSL = [B8[4].ap(), B8[5].ap(), B8[6].ap()]
        WRR = [BRES[4], BRES[5], BRES[6]]

        for li in range(NLB):
            lt = f"mb{li}"
            p.dma("sync", (lambda li=li: lambda h: h.dma_start(
                out=vcols("b_nw", 0, NT), in_=EI["b_norm_w"][li]))(),
                writes=[f"{lt}_vnw", "gb_vecs"], sem_key="wl2")
            p.dma("sync", (lambda li=li: lambda h: h.dma_start(
                out=vcols("b_nb", 0, NT), in_=EI["b_norm_b"][li]))(),
                writes=[f"{lt}_vnb", "gb_vecs"], sem_key="wl2")
            p.dma("sync", (lambda li=li: lambda h: h.dma_start(
                out=vcols("b_cb", 0, NTI), in_=EI["b_conv_b"][li]))(),
                writes=[f"{lt}_vcb", "gb_vecs"], sem_key="wl2")
            p.dma("sync", (lambda li=li: lambda h: h.dma_start(
                out=vcols("b_dtb", 0, NTI), in_=EI["b_dt_proj_b"][li]))(),
                writes=[f"{lt}_vdtb", "gb_vecs"], sem_key="wl2")
            p.dma("sync", (lambda li=li: lambda h: h.dma_start(
                out=vcols("b_D", 0, NTI), in_=EI["b_D"][li]))(),
                writes=[f"{lt}_vD", "gb_vecs"], sem_key="wl2")
            p.dma("sync", (lambda li=li: lambda h: h.dma_start(
                out=CWS[:, :], in_=EI["b_conv_w"][li]))(),
                writes=[f"{lt}_vcw", "CWS"], sem_key="wl2")
            p.dma("sync", (lambda li=li: lambda h: h.dma_start(
                out=NEAT[:, :], in_=EI["b_A_log"][li]))(),
                writes=[f"{lt}_alog", "NEA"], sem_key="wl2")
            p.op("scalar", lambda h: h.activation(
                out=NEAT[:, :], in_=NEAT[:, :], func=AF.Exp),
                reads=[f"{lt}_alog"], writes=[f"{lt}_aexp", "NEA"])
            p.op("vector", lambda h: h.tensor_scalar(
                out=NEAT[:, :], in0=NEAT[:, :], scalar1=-1.0, scalar2=None,
                op0=ALU.mult),
                reads=[f"{lt}_aexp"], writes=[f"{lt}_nea", "NEA"])

            emit_norm(resdr, "cur", f"{lt}n", "ln",
                      gamma=lambda i: vcol("b_nw", i), beta=lambda i: vcol("b_nb", i))
            hbr = [f"{lt}n_hbf{i}" for i in range(NT)]

            for g in range(8):
                slot = g % 3
                wr = f"{lt}_wip{g}"
                wload(WSL[slot].rearrange("p (kc c) -> p kc c", kc=8),
                      EI["b_in_proj_w"][li].rearrange("(kc p) c -> p kc c", p=128)
                      [:, :, g * 512:(g + 1) * 512],
                      wr, dst_res=WRR[slot])
                for oi in range(4):
                    o = g * 4 + oi
                    bk = (o * 5) % 4
                    for half in range(2):
                        def mmg(h, oi=oi, slot=slot, bk=bk, half=half):
                            ins = []
                            for cc in range(2):
                                c = half * 2 + cc
                                for kc in range(NT):
                                    ins.append(h.matmul(
                                        out=PSQ[bk][:, cc * 512:cc * 512 + 512],
                                        lhsT=WSL[slot][:, (kc * 4 + oi) * 128:
                                                       (kc * 4 + oi) * 128 + 128],
                                        rhs=hbf[kc][:, c * 512:(c + 1) * 512],
                                        start=(kc == 0), stop=(kc == NT - 1)))
                            return ins
                        p.ops("tensor", mmg, reads=[wr] + hbr, writes=[BK[bk]])
                        for cc in range(2):
                            c = half * 2 + cc
                            if o < NTI:
                                p.op("scalar", (lambda o=o, c=c, cc=cc, bk=bk:
                                                lambda h: h.activation(
                                    out=xm[o][:, c * 512:(c + 1) * 512],
                                    in_=PSQ[bk][:, cc * 512:cc * 512 + 512],
                                    func=AF.Identity))(),
                                    reads=[BK[bk]],
                                    writes=[f"{lt}_xi{o}_{c}", ARES[o // 2]])
                            else:
                                zt = o - NTI
                                p.op("scalar", (lambda c=c, cc=cc, bk=bk:
                                                lambda h: h.activation(
                                    out=GS[c % 3][:, 0:512],
                                    in_=PSQ[bk][:, cc * 512:cc * 512 + 512],
                                    func=AF.Silu))(),
                                    reads=[BK[bk]],
                                    writes=[f"{lt}_zs{zt}_{c}", f"GS{c % 3}"])
                                p.dma("sync", (lambda zt=zt, c=c: lambda h:
                                               h.dma_start(
                                    out=z_sp[zt * 128:(zt + 1) * 128,
                                             c * 512:(c + 1) * 512],
                                    in_=GS[c % 3][:, 0:512]))(),
                                    reads=[f"{lt}_zs{zt}_{c}"],
                                    writes=[f"{lt}_zd{zt}_{c}", "zsp"],
                                    sem_key="sp")

            xi3 = [f"{lt}_xi{ot}_3" for ot in range(NTI)]
            lane_write(f"{lt}h", lambda l: [
                (hl_sl[ot * 128:(ot + 1) * 128, 0:3],
                 A8[ot // 2].ap()[:, (ot % 2) * 2052 + 2049:(ot % 2) * 2052 + 2052])
                for ot in range(NTI)], xi3)
            nonce_write(f"{lt}h", [f"xw_{lt}h"])
            hz = []
            for ot in range(NTI):
                p.op("vector", (lambda ot=ot: lambda h: h.memset(xmp[ot], 0.0))(),
                     writes=[f"{lt}_hz{ot}", ARES[ot // 2]])
                hz.append(f"{lt}_hz{ot}")
            pair_poll_read(f"{lt}h", lambda l: [
                ((A8[ot // 2].ap()[:, (ot % 2) * 2052 + 1:(ot % 2) * 2052 + 4]
                  if l == 1 else b16v(X8[6])[:, ot * 4:ot * 4 + 3]),
                 hl_sl[ot * 128:(ot + 1) * 128, 0:3]) for ot in range(NTI)],
                [f"{lt}_hr"] + ([f"X6"]) + [ARES[j] for j in range(8)],
                extra_deps=hz + [f"xw_{lt}h"])

            for ot in range(NTI):
                base = (ot % 2) * 2052
                for c in (3, 2, 1, 0):
                    rdep = [f"{lt}_xi{ot}_{c}", f"{lt}_hr"] \
                        + ([f"{lt}_xi{ot}_{c-1}"] if c > 0 else [])
                    p.op("vector", (lambda ot=ot, c=c, base=base: lambda h:
                                    h.tensor_scalar(
                        out=EVa[:, 0:512],
                        in0=A8[ot // 2].ap()[:, base + 1 + c * 512:
                                             base + 1 + c * 512 + 512],
                        scalar1=CWS[:, ot * 4:ot * 4 + 1], scalar2=None,
                        op0=ALU.mult))(),
                        reads=rdep + [f"{lt}_vcw"], writes=[f"{lt}_cv{ot}_{c}a", "EVa"])
                    for j in (1, 2):
                        p.op("vector", (lambda ot=ot, c=c, j=j, base=base: lambda h:
                                        h.scalar_tensor_tensor(
                            out=EVa[:, 0:512],
                            in0=A8[ot // 2].ap()[:, base + 1 + j + c * 512:
                                                 base + 1 + j + c * 512 + 512],
                            scalar=CWS[:, ot * 4 + j:ot * 4 + j + 1],
                            in1=EVa[:, 0:512], op0=ALU.mult, op1=ALU.add))(),
                            reads=[f"{lt}_cv{ot}_{c}{'a' if j == 1 else 'b'}"],
                            writes=[f"{lt}_cv{ot}_{c}{'b' if j == 1 else 'c'}", "EVa"])
                    p.op("vector", (lambda ot=ot, c=c, base=base: lambda h:
                                    h.scalar_tensor_tensor(
                        out=EVa[:, 0:512],
                        in0=A8[ot // 2].ap()[:, base + 4 + c * 512:
                                             base + 4 + c * 512 + 512],
                        scalar=CWS[:, ot * 4 + 3:ot * 4 + 4],
                        in1=EVa[:, 0:512], op0=ALU.mult, op1=ALU.add))(),
                        reads=[f"{lt}_cv{ot}_{c}c"], writes=[f"{lt}_cv{ot}_{c}d", "EVa"])
                    p.op("scalar", (lambda ot=ot, c=c, base=base: lambda h:
                                    h.activation(
                        out=A8[ot // 2].ap()[:, base + 4 + c * 512:
                                             base + 4 + c * 512 + 512],
                        in_=EVa[:, 0:512], func=AF.Silu,
                        bias=vcol("b_cb", ot)))(),
                        reads=[f"{lt}_cv{ot}_{c}d", f"{lt}_vcb"],
                        writes=[f"{lt}_xc{ot}_{c}", ARES[ot // 2]])
                p.dma("sync", (lambda ot=ot: lambda h: h.dma_start(
                    out=xc_sp[ot * 128:(ot + 1) * 128, :], in_=xm[ot]))(),
                    reads=[f"{lt}_xc{ot}_{c2}" for c2 in range(4)],
                    writes=[f"{lt}_xcd{ot}", "xcsp"], sem_key="sp")
            xcr = [f"{lt}_xc{ot}_{c}" for ot in range(NTI) for c in range(4)]
            if li == 0:
                dbg_sbuf("mbxc", xm, [f"{lt}_xc{ot}_3" for ot in range(NTI)],
                         regions=[ARES[j] for j in range(8)] + xcr)

            p.dma("gpsimd", (lambda li=li: lambda h: h.dma_start(
                out=B8[6].ap()[:, 0:1536].rearrange("p (kc c) -> p kc c", kc=16),
                in_=EI["b_x_proj_w"][li].rearrange("(kc p) c -> p kc c", p=128)))(),
                writes=[f"{lt}_xpw", "B6"], sem_key="wl")
            for c in range(NCH):
                def mmxp(h, c=c):
                    ins = []
                    for kc in range(NTI):
                        ins.append(h.matmul(
                            out=PSQ[c // 2][0:96, (c % 2) * 512:(c % 2) * 512 + 512],
                            lhsT=B8[6].ap()[:, kc * 96:(kc + 1) * 96],
                            rhs=xm[kc][:, c * 512:(c + 1) * 512],
                            start=(kc == 0), stop=(kc == NTI - 1)))
                    return ins
                p.ops("tensor", mmxp, reads=[f"{lt}_xpw"] + xcr,
                      writes=[BK[c // 2]])
            dbc = f32v(X8[6])[0:96, 0:2048]
            for c in range(NCH):
                p.op("vector", (lambda c=c: lambda h: h.tensor_copy(
                    out=dbc[:, c * 512:(c + 1) * 512],
                    in_=PSQ[c // 2][0:96, (c % 2) * 512:(c % 2) * 512 + 512]))(),
                    reads=[BK[c // 2]], writes=[f"{lt}_dbc{c}", "X6"])
            dbcr = [f"{lt}_dbc{c}" for c in range(NCH)]
            if li == 0:
                dbg_sbuf("mbdbc", [dbc], [dbcr[3]], rows=96, regions=["X6"] + dbcr[:3])
            p.op("vector", lambda h: h.tensor_copy(
                out=b16v(X8[7])[0:64, 0:2048], in_=dbc[0:64, :]),
                reads=dbcr, writes=[f"{lt}_dtr", "X7"])
            def wbc(h, inc):
                lr = get_lane(h)
                with h.If_eq(lr, 0):
                    inc(h.dma_start(out=bc_sl[:, 0:2048], in_=dbc[64:96, :]))
                with h.Else():
                    inc(h.dma_start(out=bc_sl[:, 2048:4096], in_=dbc[64:96, :]))
            p.dma_multi("gpsimd", wbc, 1, reads=dbcr + ["barrier"],
                        writes=[f"xw_{lt}bc"], sem_key="hsk")

            p.dma("gpsimd", (lambda li=li: lambda h: h.dma_start(
                out=PBUF.ap()[0:64, 0:2048], in_=EI["b_dt_proj_w"][li]))(),
                writes=[f"{lt}_dtw", "PBUF"], sem_key="wl")
            fwd_tags = [f"xw_{lt}bc"]
            for tl in range(NTI):
                r = tl % 4
                dtst = B8[4 + r // 2].ap()[:, (r % 2) * 2048:(r % 2) * 2048 + 2048]
                dtsr = BRES[4 + r // 2]
                dtust = B8[6].ap()[:, 1536 + 1 * 0:1536 + 2048]
                bk = tl % 4
                for c in range(NCH):
                    p.op("tensor", (lambda tl=tl, c=c, bk=bk: lambda h: h.matmul(
                        out=PSQ[bk][:, (c % 2) * 512:(c % 2) * 512 + 512],
                        lhsT=PBUF.ap()[0:64, tl * 128:(tl + 1) * 128],
                        rhs=b16v(X8[7])[0:64, c * 512:(c + 1) * 512],
                        start=True, stop=True))(),
                        reads=[f"{lt}_dtw", f"{lt}_dtr"], writes=[BK[bk]])
                    p.op("scalar", (lambda tl=tl, c=c, bk=bk, dtst=dtst: lambda h:
                                    h.activation(
                        out=dtst[:, c * 512:(c + 1) * 512],
                        in_=PSQ[bk][:, (c % 2) * 512:(c % 2) * 512 + 512],
                        func=AF.Exp, bias=vcol("b_dtb", tl)))(),
                        reads=[BK[bk], f"{lt}_vdtb"],
                        writes=[f"{lt}_dte{tl}_{c}", dtsr])
                    p.op("scalar", (lambda c=c, dtst=dtst: lambda h: h.activation(
                        out=dtst[:, c * 512:(c + 1) * 512],
                        in_=dtst[:, c * 512:(c + 1) * 512],
                        func=AF.Ln, bias=onec[0:128, 0:1]))(),
                        reads=[f"{lt}_dte{tl}_{c}", "gb_vecs"],
                        writes=[f"{lt}_dtl{tl}_{c}", dtsr])
                    p.op("vector", (lambda tl=tl, c=c, dtst=dtst, dtust=dtust:
                                    lambda h: h.tensor_mul(
                        out=dtust[:, c * 512:(c + 1) * 512],
                        in0=dtst[:, c * 512:(c + 1) * 512],
                        in1=xm[tl][:, c * 512:(c + 1) * 512]))(),
                        reads=[f"{lt}_dtl{tl}_{c}", f"{lt}_xc{tl}_{c}"],
                        writes=[f"{lt}_dtu{tl}_{c}", "B6"])
                dtc = [f"{lt}_dtl{tl}_{c}" for c in range(NCH)]
                duc = [f"{lt}_dtu{tl}_{c}" for c in range(NCH)]
                lane_write(f"{lt}f{tl}", (lambda tl=tl, dtst=dtst, dtust=dtust:
                                          lambda l: [
                    (dt_sl[tl * 128:(tl + 1) * 128, l * 2048:(l + 1) * 2048], dtst),
                    (dtu_sl[tl * 128:(tl + 1) * 128, l * 2048:(l + 1) * 2048], dtust),
                ])(), dtc + duc)
                fwd_tags.append(f"xw_{lt}f{tl}")
            nonce_write(f"{lt}f", fwd_tags)

            for hf in range(2):
                ldt = []
                if hf == 0:
                    pair_poll_read(f"{lt}f", lambda l: [
                        (A8[t4].ap()[:, 0:4096],
                         dt_sl[l * 1024 + t4 * 128:l * 1024 + t4 * 128 + 128, :])
                        for t4 in range(4)] + [
                        (A8[4 + t4].ap()[:, 0:4096],
                         dtu_sl[l * 1024 + t4 * 128:l * 1024 + t4 * 128 + 128, :])
                        for t4 in range(4)],
                        [f"{lt}_F"] + [ARES[j] for j in range(8)],
                        extra_deps=fwd_tags)
                else:
                    for t4 in range(4):
                        for sl_, off in ((dt_sl, 0), (dtu_sl, 4)):
                            def mk(t4=t4, sl_=sl_, off=off):
                                def emit(h, inc):
                                    lr = get_lane(h)
                                    with h.If_eq(lr, 0):
                                        inc(h.dma_start(
                                            out=A8[off + t4].ap()[:, 0:4096],
                                            in_=sl_[512 + t4 * 128:512 + t4 * 128 + 128, :]))
                                    with h.Else():
                                        inc(h.dma_start(
                                            out=A8[off + t4].ap()[:, 0:4096],
                                            in_=sl_[1024 + 512 + t4 * 128:
                                                    1024 + 512 + t4 * 128 + 128, :]))
                                return emit
                            p.dma_multi("gpsimd", mk(), 1,
                                        reads=[f"{lt}_F"],
                                        writes=[f"{lt}_l2{t4}_{off}",
                                                ARES[off + t4]], sem_key="hsk")
                            ldt.append(f"{lt}_l2{t4}_{off}")
                ldtags = [f"{lt}_F"] if hf == 0 else ldt
                for j in range(8):
                    p.op("vector", (lambda j=j: lambda h: h.memset(
                        f32v(X8[j])[:, 0:2048], 0.0))(),
                        writes=[f"{lt}_y0_{hf}_{j}", f"X{j}"])
                ytags = [f"{lt}_y0_{hf}_{j}" for j in range(8)]
                for s in range(16):
                    p.dma("sync", (lambda s=s: lambda h: h.dma_start(
                        out=B8[4].ap()[0:1, 0:4096], in_=bc_sl[s:s + 1, :]))(),
                        reads=[f"{lt}_F"], writes=[f"{lt}_br{hf}_{s}", "B4"],
                        sem_key="xs")
                    p.dma("sync", (lambda s=s: lambda h: h.dma_start(
                        out=B8[5].ap()[0:1, 0:4096], in_=bc_sl[16 + s:17 + s, :]))(),
                        reads=[f"{lt}_F"], writes=[f"{lt}_cr{hf}_{s}", "B5"],
                        sem_key="xs")
                    def mkbr(h, s=s, hf=hf):
                        ins = []
                        for q in range(4):
                            for cc in range(2):
                                ins.append(h.matmul(
                                    out=PSQ[q][:, cc * 512:cc * 512 + 512],
                                    lhsT=ONER[:, :],
                                    rhs=B8[4].ap()[0:1, q * 1024 + cc * 512:
                                                   q * 1024 + cc * 512 + 512],
                                    start=True, stop=True))
                        return ins
                    p.ops("tensor", mkbr, reads=[f"{lt}_br{hf}_{s}", "ONER"],
                          writes=[BK[0], BK[1], BK[2], BK[3]])
                    for t4 in range(4):
                        gt = hf * 4 + t4
                        p.op("scalar", (lambda t4=t4, gt=gt, s=s: lambda h:
                                        h.activation(
                            out=B8[6].ap()[:, 0:4096], in_=A8[t4].ap()[:, 0:4096],
                            func=AF.Exp,
                            scale=NEAT[:, gt * 16 + s:gt * 16 + s + 1]))(),
                            reads=[ldtags[0] if hf == 0 else ldt[t4 * 2],
                                   f"{lt}_nea"] + ([] if hf == 0 else []),
                            writes=[f"{lt}_a{hf}_{s}_{t4}", "B6"])
                        def mkw(h, t4=t4):
                            ins = []
                            for q in range(4):
                                ins.append(h.tensor_mul(
                                    out=b16v(PBUF)[:, q * 1024:(q + 1) * 1024],
                                    in0=A8[4 + t4].ap()[:, q * 1024:(q + 1) * 1024],
                                    in1=PSQ[q][:, 0:1024]))
                            return ins
                        p.ops("vector", mkw,
                              reads=[BK[0], BK[1], BK[2], BK[3]] +
                                    ([f"{lt}_F"] if hf == 0 else
                                     [ldt[t4 * 2 + 1]]),
                              writes=[f"{lt}_w{hf}_{s}_{t4}", "PBUF"])
                        p.op("vector", (lambda t4=t4, s=s, hf=hf: lambda h:
                                        h.tensor_tensor_scan(
                            out=B8[t4].ap()[:, 0:4096],
                            data0=B8[6].ap()[:, 0:4096],
                            data1=b16v(PBUF)[:, 0:4096],
                            initial=zcol[0:128, 0:1],
                            op0=ALU.mult, op1=ALU.add))(),
                            reads=[f"{lt}_a{hf}_{s}_{t4}", f"{lt}_w{hf}_{s}_{t4}",
                                   "gb_vecs"],
                            writes=[f"{lt}_h{hf}_{s}_{t4}", BRES[t4]])
                    def mkcr(h, s=s, hf=hf):
                        ins = []
                        for q in range(4):
                            for cc in range(2):
                                ins.append(h.matmul(
                                    out=PSQ[q][:, cc * 512:cc * 512 + 512],
                                    lhsT=ONER[:, :],
                                    rhs=B8[5].ap()[0:1, q * 1024 + cc * 512:
                                                   q * 1024 + cc * 512 + 512],
                                    start=True, stop=True))
                        return ins
                    p.ops("tensor", mkcr,
                          reads=[f"{lt}_cr{hf}_{s}", "ONER"] +
                                [f"{lt}_w{hf}_{s}_{t4}" for t4 in range(4)],
                          writes=[BK[0], BK[1], BK[2], BK[3]])
                    for t4 in range(4):
                        def mkt1(h, t4=t4):
                            ins = []
                            for q in range(4):
                                ins.append(h.tensor_mul(
                                    out=B8[6].ap()[:, q * 1024:(q + 1) * 1024],
                                    in0=B8[t4].ap()[:, q * 1024:(q + 1) * 1024],
                                    in1=PSQ[q][:, 0:1024]))
                            return ins
                        p.ops("vector", mkt1,
                              reads=[BK[0], BK[1], BK[2], BK[3],
                                     f"{lt}_h{hf}_{s}_{t4}"],
                              writes=[f"{lt}_t1{hf}_{s}_{t4}", "B6"])
                        for cc in range(2):
                            p.op("vector", (lambda t4=t4, cc=cc: lambda h:
                                            h.tensor_add(
                                out=f32v(X8[2 * t4 + cc])[:, 0:2048],
                                in0=f32v(X8[2 * t4 + cc])[:, 0:2048],
                                in1=b16v(B8[6])[:, cc * 2048:(cc + 1) * 2048]))(),
                                reads=[f"{lt}_t1{hf}_{s}_{t4}"] + ytags,
                                writes=[f"{lt}_ya{hf}_{s}_{t4}_{cc}",
                                        f"X{2 * t4 + cc}"])
                    ytags = [f"{lt}_ya{hf}_{s}_{t4}_{cc}"
                             for t4 in range(4) for cc in range(2)]
                lane_write(f"{lt}y{hf}", (lambda hf=hf: lambda l: [
                    (y_sl[l * 1024 + (hf * 4 + t4) * 128:
                          l * 1024 + (hf * 4 + t4) * 128 + 128,
                          cc * 2048:(cc + 1) * 2048],
                     f32v(X8[2 * t4 + cc])[:, 0:2048])
                    for t4 in range(4) for cc in range(2)])(), ytags)
            nonce_write(f"{lt}y", [f"xw_{lt}y0", f"xw_{lt}y1"])

            pair_poll_read(f"{lt}y", lambda l: [
                (ym[ot], y_sl[ot * 128:(ot + 1) * 128, l * 2048:(l + 1) * 2048])
                for ot in range(NTI)],
                [f"{lt}_Y"] + [ARES[j] for j in range(8)],
                extra_deps=[f"xw_{lt}y0", f"xw_{lt}y1"])
            if li == 0:
                dbg_sbuf("mby", ym, [f"{lt}_Y"] * NTI,
                         regions=[ARES[j] for j in range(8)])
            for ot in range(NTI):
                xcb = b16v(X8[ot % 2])[:, 0:2048]
                szb = b16v(X8[2 + ot % 2])[:, 0:2048]
                p.dma("sync", (lambda ot=ot, xcb=xcb: lambda h: h.dma_start(
                    out=xcb, in_=xc_sp[ot * 128:(ot + 1) * 128, :]))(),
                    reads=[f"{lt}_xcd{ot}", "xcsp"],
                    writes=[f"{lt}_gx{ot}", f"X{ot % 2}"], sem_key="xs")
                p.dma("sync", (lambda ot=ot, szb=szb: lambda h: h.dma_start(
                    out=szb, in_=z_sp[ot * 128:(ot + 1) * 128, :]))(),
                    reads=[f"{lt}_zd{ot}_0", f"{lt}_zd{ot}_1", f"{lt}_zd{ot}_2",
                           f"{lt}_zd{ot}_3", "zsp"],
                    writes=[f"{lt}_gz{ot}", f"X{2 + ot % 2}"], sem_key="xs")
                p.op("vector", (lambda ot=ot, xcb=xcb: lambda h:
                                h.scalar_tensor_tensor(
                    out=ym[ot], in0=xcb, scalar=vcol("b_D", ot),
                    in1=ym[ot], op0=ALU.mult, op1=ALU.add))(),
                    reads=[f"{lt}_gx{ot}", f"{lt}_Y", f"{lt}_vD"],
                    writes=[f"{lt}_yd{ot}", ARES[ot // 2]])
                p.op("vector", (lambda ot=ot, szb=szb: lambda h: h.tensor_mul(
                    out=ym[ot], in0=ym[ot], in1=szb))(),
                    reads=[f"{lt}_yd{ot}", f"{lt}_gz{ot}"],
                    writes=[f"{lt}_y2{ot}", ARES[ot // 2]])
            y2r = [f"{lt}_y2{ot}" for ot in range(NTI)]
            for oo in range(NT):
                slot = oo % 3
                wr = f"{lt}_wop{oo}"
                wload(WSL[slot][:, 0:2048].rearrange("p (kc c2) -> p kc c2", kc=16),
                      EI["b_out_proj_wo"][li, oo].rearrange("kc p c2 -> p kc c2"),
                      wr, dst_res=WRR[slot])
                for c in range(NCH):
                    bk = (oo + c) % 4

                    def mmop(h, oo=oo, c=c, slot=slot, bk=bk):
                        ins = []
                        for kc in range(NTI):
                            ins.append(h.matmul(
                                out=PSQ[bk][:, 0:512],
                                lhsT=WSL[slot][:, kc * 128:(kc + 1) * 128],
                                rhs=ym[kc][:, c * 512:(c + 1) * 512],
                                start=(kc == 0), stop=(kc == NTI - 1)))
                        return ins
                    p.ops("tensor", mmop, reads=[wr] + y2r, writes=[BK[bk]])
                    ev = EVb if (oo + c) % 2 == 0 else EVa
                    evr = "EVb" if (oo + c) % 2 == 0 else "EVa"
                    p.dma("sync", (lambda oo=oo, c=c, ev=ev: lambda h: h.dma_start(
                        out=ev[:, 0:512],
                        in_=resdr[oo * 128:(oo + 1) * 128, c * 512:(c + 1) * 512]))(),
                        reads=["cur"], writes=[f"{lt}_ro{oo}_{c}", evr],
                        sem_key="xs")
                    p.op("vector", (lambda oo=oo, c=c, bk=bk, ev=ev: lambda h:
                                    h.tensor_add(
                        out=ev[:, 0:512], in0=ev[:, 0:512],
                        in1=PSQ[bk][:, 0:512]))(),
                        reads=[f"{lt}_ro{oo}_{c}", BK[bk]],
                        writes=[f"{lt}_rn{oo}_{c}", evr])
                    p.dma("sync", (lambda oo=oo, c=c, ev=ev: lambda h: h.dma_start(
                        out=resdr[oo * 128:(oo + 1) * 128, c * 512:(c + 1) * 512],
                        in_=ev[:, 0:512]))(),
                        reads=[f"{lt}_rn{oo}_{c}"],
                        writes=["cur", f"{lt}_ru{oo}_{c}"], sem_key="xs")
            if li == 0:
                dbg_dram("mbres0", resdr, "cur")

        p.dma("sync", lambda h: h.dma_start(
            out=vcols("b_nw", 0, NT), in_=EI["b_normf_w"]),
            writes=["mbf_vnw", "gb_vecs"], sem_key="wl2")
        p.dma("sync", lambda h: h.dma_start(
            out=vcols("b_nb", 0, NT), in_=EI["b_normf_b"]),
            writes=["mbf_vnb", "gb_vecs"], sem_key="wl2")
        emit_norm(resdr, "cur", "mbf", "ln",
                  gamma=lambda i: vcol("b_nw", i), beta=lambda i: vcol("b_nb", i))
        for i in range(NT):
            for c in range(NCH):
                ev = EVb if (i + c) % 2 == 0 else EVa
                evr = "EVb" if (i + c) % 2 == 0 else "EVa"
                p.dma("sync", (lambda i=i, c=c, ev=ev: lambda h: h.dma_start(
                    out=ev[:, 0:512],
                    in_=xdr[i * 128:(i + 1) * 128, c * 512:(c + 1) * 512]))(),
                    reads=["xdr"], writes=[f"mbf_xo{i}_{c}", evr], sem_key="xs")
                p.op("vector", (lambda i=i, c=c, ev=ev: lambda h: h.tensor_add(
                    out=ev[:, 0:512], in0=ev[:, 0:512],
                    in1=hbf[i][:, c * 512:(c + 1) * 512]))(),
                    reads=[f"mbf_xo{i}_{c}", f"mbf_hbf{i}"],
                    writes=[f"mbf_xn{i}_{c}", evr])
                p.dma("sync", (lambda i=i, c=c, ev=ev: lambda h: h.dma_start(
                    out=xdr[i * 128:(i + 1) * 128, c * 512:(c + 1) * 512],
                    in_=ev[:, 0:512]))(),
                    reads=[f"mbf_xn{i}_{c}"], writes=["xdr", f"mbf_xu{i}_{c}"],
                    sem_key="xs")

    if n_stages >= 1:
        stage1()
    if n_stages >= 2:
        emit_ffn(0)
    if n_stages >= 3:
        stage3()
    if n_stages >= 4:
        emit_ffn(1)

    for i in range(NT):
        p.dma("sync", (lambda i=i: lambda h: h.dma_start(
            out=xout[i * 128:(i + 1) * 128, :], in_=xdr[i * 128:(i + 1) * 128, :]))(),
            reads=["xdr"], writes=[f"xo{i}"], sem_key="out")
    p.barrier_wait("sync")
    ctx.close()
    return nc



def _prep_inputs(inputs):
    f = np.float32
    x = np.asarray(inputs["L0_attn_a_x"], f)
    n1 = np.asarray(inputs["L0_attn_a_norm1_w"], f)
    n2 = np.asarray(inputs["L0_attn_a_norm2_w"], f)
    ipw = np.asarray(inputs["L0_attn_a_in_proj_w"], f)
    ipb = np.asarray(inputs["L0_attn_a_in_proj_b"], f)
    cw = np.asarray(inputs["L0_attn_a_conv_w"], f)[:, 0, :]
    cb = np.asarray(inputs["L0_attn_a_conv_b"], f)
    dtw = np.asarray(inputs["L0_attn_a_dt_w"], f)
    dtb = np.asarray(inputs["L0_attn_a_dt_b"], f)
    bpw = np.asarray(inputs["L0_attn_a_bp_w"], f)
    bpb = np.asarray(inputs["L0_attn_a_bp_b"], f)
    cpw = np.asarray(inputs["L0_attn_a_cp_w"], f)
    cpb = np.asarray(inputs["L0_attn_a_cp_b"], f)
    s2iw = np.asarray(inputs["L0_attn_a_s2i_w"], f)
    s2ib = np.asarray(inputs["L0_attn_a_s2i_b"], f)
    Dp = np.asarray(inputs["L0_attn_a_D_param"], f)
    outw = np.asarray(inputs["L0_attn_a_out_w"], f)
    outb = np.asarray(inputs["L0_attn_a_out_b"], f)
    gw = np.asarray(inputs["L0_attn_a_gate_w"], f)
    gb = np.asarray(inputs["L0_attn_a_gate_b"], f)
    ew1 = np.asarray(inputs["L0_attn_a_e_w1"], f)
    eb1 = np.asarray(inputs["L0_attn_a_e_b1"], f)
    ew2 = np.asarray(inputs["L0_attn_a_e_w2"], f)
    eb2 = np.asarray(inputs["L0_attn_a_e_b2"], f)

    def pmaj(v, nt):
        return np.ascontiguousarray(v.reshape(nt, 128).T)

    dtbc = np.concatenate([dtw, bpw, cpw], axis=1)
    dtbc_b1 = np.concatenate([dtb, bpb])[:, None]
    dtbc_b2 = np.concatenate([cpb, np.zeros(64, f)])[:, None]
    s2ib_w = np.concatenate([s2iw, s2ib[None, :]], axis=0)
    out_wo = np.ascontiguousarray(
        outw.reshape(NTI, 128, NT, 128).transpose(2, 0, 1, 3))
    ew2o = np.ascontiguousarray(
        ew2.reshape(NEXP, 32, 128, NT, 128).transpose(0, 3, 1, 2, 4))
    common = dict(
        identity_bf=np.eye(128, dtype=BF16NP),
        IDNF=np.eye(128, dtype=f),
        a_norm1_w=pmaj(n1, NT), a_norm2_w=pmaj(n2, NT),
        a_in_proj_w=ipw,
        a_in_proj_b=pmaj(ipb, 2 * NTI),
        a_conv_w=np.ascontiguousarray(cw.reshape(NTI, 128, 3).transpose(1, 0, 2)
                                      .reshape(128, NTI * 3)),
        a_conv_b=pmaj(cb, NTI),
        a_dtbc_w=dtbc, a_dtbc_b1=dtbc_b1, a_dtbc_b2=dtbc_b2,
        a_s2ib_w=s2ib_w, a_D=pmaj(Dp, NTI),
        a_out_wo=out_wo, a_out_b=pmaj(outb, NT),
        a_gate_w=np.ascontiguousarray(gw.reshape(NT, 128, NEXP).transpose(1, 0, 2)
                                      .reshape(128, NT * NEXP)),
        a_gate_b=gb[None, :],
        a_e_w1=ew1,
        a_e_b1=np.ascontiguousarray(eb1.reshape(NEXP, 32, 128).transpose(0, 2, 1)),
        a_e_w2o=ew2o,
        a_e_b2=np.ascontiguousarray(eb2.reshape(NEXP, NT, 128).transpose(0, 2, 1)),
    )
    NNODE, NLEAF, FDEP = 15, 16, 4
    selmat = np.zeros((45, 96), f)
    negthr = np.full((128, 1), -1e30, f)
    for fidx in range(3):
        for lf in range(NLEAF):
            cur = NNODE + lf
            nzero = 0
            for _ in range(FDEP):
                parent = (cur - 1) // 2
                bit = (cur - 1) % 2
                selmat[fidx * 15 + parent, fidx * 32 + lf] = 1.0 if bit else -1.0
                if bit == 0:
                    nzero += 1
                cur = parent
            negthr[fidx * 32 + lf, 0] = nzero - 3.5
    negthr[96:128] = 0.0
    nodew = np.concatenate([np.asarray(inputs[f"L0_ffn_{ff}_node_w"], f)
                            for ff in ("q", "k", "v")], 0)
    nodeb = np.concatenate([np.asarray(inputs[f"L0_ffn_{ff}_node_b"], f)
                            for ff in ("q", "k", "v")], 0)
    nodeb_p = np.zeros((128, 1), f)
    nodeb_p[:45, 0] = nodeb
    common.update(
        f_selmat=selmat, f_negthr=negthr,
        f_node_w=np.ascontiguousarray(nodew.T), f_node_b=nodeb_p)
    for ff in ("q", "k", "v"):
        fw1 = np.asarray(inputs[f"L0_ffn_{ff}_w1"], f)
        fb1 = np.asarray(inputs[f"L0_ffn_{ff}_b1"], f).reshape(1024)
        fw2 = np.asarray(inputs[f"L0_ffn_{ff}_w2"], f)
        fb2 = np.asarray(inputs[f"L0_ffn_{ff}_b2"], f)
        common[f"f_{ff}_w1"] = np.ascontiguousarray(
            fw1.transpose(1, 0, 2).reshape(1024, 1024))
        common[f"f_{ff}_b1"] = pmaj(fb1, 8)
        common[f"f_{ff}_w2"] = np.ascontiguousarray(fw2.reshape(1024, 1024))
        common[f"f_{ff}_b2"] = fb2

    bnw = np.asarray(inputs["L0_attn_b_norm_w"], f)
    bnb = np.asarray(inputs["L0_attn_b_norm_b"], f)
    bipw = np.asarray(inputs["L0_attn_b_in_proj_w"], f)
    bcw = np.asarray(inputs["L0_attn_b_conv_w"], f)
    bcb = np.asarray(inputs["L0_attn_b_conv_b"], f)
    bxpw = np.asarray(inputs["L0_attn_b_x_proj_w"], f)
    bdtw = np.asarray(inputs["L0_attn_b_dt_proj_w"], f)
    bdtb = np.asarray(inputs["L0_attn_b_dt_proj_b"], f)
    bAl = np.asarray(inputs["L0_attn_b_A_log"], f)
    bDp = np.asarray(inputs["L0_attn_b_D"], f)
    bopw = np.asarray(inputs["L0_attn_b_out_proj_w"], f)
    bnfw = np.asarray(inputs["L0_attn_b_normf_w"], f)
    bnfb = np.asarray(inputs["L0_attn_b_normf_b"], f)
    common.update(
        b_norm_w=np.stack([pmaj(bnw[i], NT) for i in range(NLB)]),
        b_norm_b=np.stack([pmaj(bnb[i], NT) for i in range(NLB)]),
        b_in_proj_w=np.ascontiguousarray(bipw.transpose(0, 2, 1)),
        b_conv_w=np.ascontiguousarray(
            bcw.reshape(NLB, NTI, 128, 4).transpose(0, 2, 1, 3)
            .reshape(NLB, 128, NTI * 4)),
        b_conv_b=np.stack([pmaj(bcb[i], NTI) for i in range(NLB)]),
        b_x_proj_w=np.ascontiguousarray(bxpw.transpose(0, 2, 1)),
        b_dt_proj_w=np.ascontiguousarray(bdtw.transpose(0, 2, 1)),
        b_dt_proj_b=np.stack([pmaj(bdtb[i], NTI) for i in range(NLB)]),
        b_D=np.stack([pmaj(bDp[i], NTI) for i in range(NLB)]),
        b_out_proj_wo=np.ascontiguousarray(
            bopw.transpose(0, 2, 1).reshape(NLB, NTI, 128, NT, 128)
            .transpose(0, 3, 1, 2, 4)),
        b_normf_w=pmaj(bnfw, NT), b_normf_b=pmaj(bnfb, NT),
    )
    b_A_lane = [np.ascontiguousarray(
        bAl[:, lane * 1024:(lane + 1) * 1024, :]
        .reshape(NLB, 8, 128, 16).transpose(0, 2, 1, 3).reshape(NLB, 128, 128))
        for lane in (0, 1)]

    in_maps = []
    for c in range(NCORES):
        b, lane = c // 2, c % 2
        xs = x[b, lane * TLOC:(lane + 1) * TLOC, :]
        m = dict(common)
        m["x_fm"] = np.ascontiguousarray(xs.T)
        if lane == 1:
            m["x_halo"] = np.ascontiguousarray(x[b, TLOC - 3:TLOC, :].T)
            m["a_in_proj_bh"] = pmaj(ipb[:INNER], NTI)
        else:
            m["x_halo"] = np.zeros((D, 3), f)
            m["a_in_proj_bh"] = np.zeros((128, NTI), f)
        m["b_A_log"] = b_A_lane[lane]
        m["lane"] = np.array([[lane]], np.int32)
        m["lane_f"] = np.full((128, 1), float(lane), f)
        in_maps.append(m)
    return in_maps


def kernel(**inputs):
    in_maps = _prep_inputs(inputs)
    nc = build_program(N_STAGES)
    res = run_bass_kernel_spmd(nc, in_maps, core_ids=list(range(NCORES)))
    x = np.asarray(inputs["L0_attn_a_x"], np.float32)
    out = np.empty_like(x)
    for c in range(NCORES):
        b, lane = c // 2, c % 2
        out[b, lane * TLOC:(lane + 1) * TLOC, :] = \
            np.asarray(res.results[c]["x_out"]).T
    return out
